# version 02
# baseline (speedup 1.0000x reference)
import numpy as np
import ml_dtypes
import concourse.bass as bass
import concourse.mybir as mybir
from concourse.bass_utils import run_bass_kernel_spmd

F32 = mybir.dt.float32
BF16 = mybir.dt.bfloat16
I32 = mybir.dt.int32
AF = mybir.ActivationFunctionType
ALU = mybir.AluOpType
AX = mybir.AxisListType

LN_EPS = 1e-5
DEN_EPS = 1e-6
WQ_EPS = 1e-5
AQ_EPS = 1e-5


class Cfg:
    def __init__(self, B_local=4, T_seq=2048, D=1024, INNER=4096):
        self.B_local = B_local
        self.T_seq = T_seq
        self.D = D
        self.INNER = INNER
        self.DC = D // 128
        self.IC = INNER // 128
        self.T_TILE = 256
        self.NTC = self.T_TILE // 128
        self.TILES_PER_SEQ = T_seq // self.T_TILE
        self.N_TILES = self.B_local * self.TILES_PER_SEQ


def build_program(cfg: Cfg):
    from contextlib import ExitStack
    nc = bass.Bass("TRN2", target_bir_lowering=False)
    stack = ExitStack()
    D, DC, IC, INNER = cfg.D, cfg.DC, cfg.IC, cfg.INNER
    NTC, T_TILE = cfg.NTC, cfg.T_TILE
    NOS = INNER // 512

    x_d = nc.dram_tensor("L0_attn_x", [cfg.B_local, cfg.T_seq, D], F32, kind="ExternalInput").ap()
    y_d = nc.dram_tensor("y", [cfg.B_local, cfg.T_seq, D], F32, kind="ExternalOutput").ap()
    wqkv_d = nc.dram_tensor("L0_attn_w_qkv", [D, 3 * D], F32, kind="ExternalInput").ap()
    bqkv_d = nc.dram_tensor("L0_attn_b_qkv", [3 * D], F32, kind="ExternalInput").ap()
    wgate_d = nc.dram_tensor("L0_attn_w_gate", [D, D], F32, kind="ExternalInput").ap()
    bgate_d = nc.dram_tensor("L0_attn_b_gate", [D], F32, kind="ExternalInput").ap()
    wproj_d = nc.dram_tensor("L0_attn_w_proj", [D, D], F32, kind="ExternalInput").ap()
    bproj_d = nc.dram_tensor("L0_attn_b_proj", [D], F32, kind="ExternalInput").ap()
    lng_d = nc.dram_tensor("L0_attn_ln_g", [D], F32, kind="ExternalInput").ap()
    lnb_d = nc.dram_tensor("L0_attn_ln_b", [D], F32, kind="ExternalInput").ap()
    fg_d = nc.dram_tensor("L0_ffn_w_gate", [INNER, D], F32, kind="ExternalInput").ap()
    fu_d = nc.dram_tensor("L0_ffn_w_up", [INNER, D], F32, kind="ExternalInput").ap()
    fd_d = nc.dram_tensor("L0_ffn_w_down", [D, INNER], F32, kind="ExternalInput").ap()

    qkB = nc.dram_tensor("qkB", [2 * DC, 128, D], BF16, kind="Internal").ap()
    gateB = nc.dram_tensor("gateB", [DC, 128, D], BF16, kind="Internal").ap()
    vR = nc.dram_tensor("vR", [DC, 128, D], BF16, kind="Internal").ap()
    projR = nc.dram_tensor("projR", [DC, 128, D], BF16, kind="Internal").ap()
    wgR = nc.dram_tensor("wgR", [DC, 128, INNER], BF16, kind="Internal").ap()
    wuR = nc.dram_tensor("wuR", [DC, 128, INNER], BF16, kind="Internal").ap()
    wdR = nc.dram_tensor("wdR", [IC, 128, D], BF16, kind="Internal").ap()

    mask_np = np.triu(np.ones((128, 128), np.float32)).astype(ml_dtypes.bfloat16)
    mask_d = nc.inline_tensor(mask_np, name="cmask_c").ap()
    idb_d = nc.inline_tensor(np.eye(128, dtype=np.float32).astype(ml_dtypes.bfloat16), name="idb_c").ap()
    ones_col_b_d = nc.inline_tensor(np.ones((128, 1), np.float32).astype(ml_dtypes.bfloat16), name="onescb").ap()
    ones_row_f_d = nc.inline_tensor(np.ones((1, 128), np.float32), name="onesrf").ap()
    ones_col_f_d = nc.inline_tensor(np.ones((128, 1), np.float32), name="onescf").ap()
    CONST_VALS = [LN_EPS, DEN_EPS, -1.0, 0.5, WQ_EPS, 1.0, 0.0, 127.0]
    consts_np = np.repeat(np.array([CONST_VALS], np.float32), 128, axis=0)
    consts_d = nc.inline_tensor(consts_np, name="consts_c").ap()

    ctx = stack.enter_context
    A = nc.alloc_sbuf_tensor

    consts = A("consts_s", [128, 8], F32)
    mask = A("mask", [128, 128], BF16)
    idb = A("idb_s", [128, 128], BF16)
    ones_col_b = A("ones_col_b", [128, 1], BF16)
    ones_row_f = A("ones_row_f", [1, 128], F32)
    ones_col_f = A("ones_col_f", [128, 1], F32)
    biases = A("biases", [128, 40], F32)
    bq_cols = A("bq_cols", [128, 3 * DC], F32)
    bg_cols = A("bg_cols", [128, DC], F32)
    lngb = A("lngb", [128, D], BF16)
    lnbb = A("lnbb", [128, D], BF16)
    bvb = A("bvb", [128, D], BF16)
    bpb = A("bpb", [128, D], BF16)
    mhat = A("mhat", [128, 3], F32)
    wscale = A("wscale", [128, 1], F32)
    wabs = A("wabs", [128, 33], F32)
    wsc = A("wsc", [1, 8], F32)
    stats = A("stats", [128, 32], F32)
    den_r = A("den_r", [128, 2], F32)
    ist = lambda t, c: stats[:, 16 * t + c:16 * t + c + 1]

    x_tok = [A(f"x_tok{t}", [128, D], F32) for t in range(NTC)]
    xn_tok = [A(f"xn_tok{t}", [128, D], F32) for t in range(NTC)]
    xn2_tok = [A(f"xn2_tok{t}", [128, D], BF16) for t in range(NTC)]
    xq_tok = [A(f"xq_tok{t}", [128, D], BF16) for t in range(NTC)]
    out_tok = x_tok
    attnx_tok = xn_tok

    fmA = A("fmA", [128, 8192], BF16)
    xn2_fm = fmA[:, 0:2048]
    gate_fm = fmA[:, 2048:4096]
    q_fm = fmA[:, 4096:6144]
    attn_fm = fmA[:, 6144:8192]
    hq_fm = fmA[:, 0:8192]
    wtern = fmA[:, 4096:8192]
    xq_fm = A("xq_fm", [128, 2048], BF16)
    k_fm = A("k_fm", [128, 2048], BF16)
    attn_tok = [k_fm[:, 0:1024], k_fm[:, 1024:2048]]
    kT = [A(f"kT{t}", [128, D], BF16) for t in range(NTC)]
    vT = [A(f"vT{t}", [128, D], BF16) for t in range(NTC)]

    S_st = A("S_st", [128, DC * D], F32)
    S_bf = A("S_bf", [128, DC * D], BF16)
    sv_st = A("sv_st", [128, DC], F32)
    sv_bf = A("sv_bf", [128, DC], BF16)

    H_tok = [A(f"H_tok{t}", [128, INNER], F32) for t in range(NTC)]
    i32s = A("i32s", [128, 4096], I32)
    hq_tok = [A(f"hq_tok{t}", [128, INNER], BF16) for t in range(NTC)]
    sg_t = A("sg_t", [128, 2048], F32)
    tmp_tok = sg_t[:, 0:D]
    wtile = H_tok[0]
    wdummy = H_tok[1]
    rowv = H_tok[0][0:1, :]
    sTm = xq_tok[0][:, 0:384]

    NSLAB = 11
    slab = [A(f"slab{i}", [128, 1024], BF16) for i in range(NSLAB)]

    trB = ctx(nc.psum_tensor("trB", [128, 1024], BF16))
    PW = [ctx(nc.psum_tensor(f"pw{i}", [128, 512], F32)) for i in range(6)]
    psmall = ctx(nc.psum_tensor("psmall", [128, 512], F32))

    class Sem:
        def __init__(self, name):
            self.h = stack.enter_context(nc.semaphore(name))
            self.n = 0

        def inc(self, instr, k=1):
            instr.then_inc(self.h, k)
            self.n += k
            return instr

    sd = Sem("sd")
    sgp = Sem("sgp")
    sv_ = Sem("sv")
    ss = Sem("ss")
    st_ = Sem("st")

    TEN, VEC, SCA, SYN, GPS = nc.tensor, nc.vector, nc.scalar, nc.sync, nc.gpsimd

    for i, v in enumerate(CONST_VALS):
        nc.const_aps.register(F32, v, consts[:, i:i + 1])
    cEPS = consts[:, 0:1]

    def wt(eng, *sems):
        for s in sems:
            if s.n > 0:
                eng.wait_ge(s.h, s.n)

    def vd():
        return VEC.drain()

    for dst, src in [(consts, consts_d), (mask, mask_d), (idb, idb_d),
                     (ones_col_b, ones_col_b_d), (ones_row_f, ones_row_f_d),
                     (ones_col_f, ones_col_f_d)]:
        sd.inc(SYN.dma_start(dst[:, :], src), 16)
    with nc.allow_non_contiguous_dma("tiny bias transposes"):
        sd.inc(SYN.dma_start(bq_cols[:, :], bqkv_d.rearrange("(c p) -> p c", p=128)), 16)
        sd.inc(SYN.dma_start(bg_cols[:, :], bgate_d.rearrange("(c p) -> p c", p=128)), 16)
    sd.inc(SYN.dma_start(rowv[:, 0:D], lng_d.rearrange("(o d) -> o d", o=1)), 16)
    sd.inc(SYN.dma_start(rowv[:, D:2 * D], lnb_d.rearrange("(o d) -> o d", o=1)), 16)
    sd.inc(SYN.dma_start(rowv[:, 2 * D:3 * D], bqkv_d[2 * D:3 * D].rearrange("(o d) -> o d", o=1)), 16)
    sd.inc(SYN.dma_start(rowv[:, 3 * D:4 * D], bproj_d.rearrange("(o d) -> o d", o=1)), 16)

    wt(VEC, sd)
    VEC.tensor_copy(biases[:, 0:DC], bq_cols[:, 0:DC])
    VEC.tensor_copy(biases[:, 8:8 + DC], bq_cols[:, DC:2 * DC])
    VEC.tensor_scalar(biases[:, 16:16 + DC], bg_cols[:, :], 0.5, None, ALU.mult)
    sv_.inc(VEC.tensor_scalar(biases[:, 24:24 + DC], bq_cols[:, 0:DC], -1.0, None, ALU.mult))

    def bcast_row(dst, col0):
        wt(TEN, sd, sv_, ss)
        TEN.matmul(PW[0][:, :], ones_row_f[:, :], rowv[:, col0:col0 + 512], start=True, stop=True)
        st_.inc(TEN.matmul(PW[1][:, :], ones_row_f[:, :], rowv[:, col0 + 512:col0 + 1024],
                           start=True, stop=True))
        wt(VEC, st_)
        VEC.tensor_copy(dst[:, 0:512], PW[0][:, :])
        sv_.inc(VEC.tensor_copy(dst[:, 512:1024], PW[1][:, :]))

    bcast_row(lngb, 0)
    bcast_row(lnbb, D)
    bcast_row(bvb, 2 * D)
    bcast_row(bpb, 3 * D)

    def conv_blocks(src_ap, in_chunks, out_cols, dst):
        for j in range(in_chunks):
            wt(SYN, sgp, st_, ss)
            sd.inc(SYN.dma_start(wtile[:, 0:out_cols], src_ap[j * 128:(j + 1) * 128, :]), 16)
            wt(VEC, sd, ss)
            sv_.inc(VEC.tensor_copy(wtern[:, 0:out_cols], wtile[:, 0:out_cols]))
            wt(GPS, sv_)
            dv = dst[:, :, j * 128:(j + 1) * 128]
            sgp.inc(GPS.dma_start(dv.rearrange("c p f -> p c f"),
                                  wtern[:, 0:out_cols].rearrange("p (c f) -> p c f", f=128)), 16)

    def conv_rows(src_ap, in_chunks, out_cols, dst):
        for j in range(in_chunks):
            wt(SYN, sgp, st_, ss)
            sd.inc(SYN.dma_start(wtile[:, 0:out_cols], src_ap[j * 128:(j + 1) * 128, :]), 16)
            wt(VEC, sd, ss)
            sv_.inc(VEC.tensor_copy(wtern[:, 0:out_cols], wtile[:, 0:out_cols]))
            wt(GPS, sv_)
            sgp.inc(GPS.dma_start(dst[j], wtern[:, 0:out_cols]), 16)

    conv_blocks(wqkv_d[:, 0:2 * D], DC, 2 * D, qkB)
    conv_blocks(wgate_d, DC, D, gateB)
    conv_rows(wqkv_d[:, 2 * D:3 * D], DC, D, vR)
    conv_rows(wproj_d, DC, D, projR)

    def quant_matrix(src_flat_tiles, n_elem, mhat_col):
        ntile = n_elem // (128 * 4096)
        for r in range(ntile):
            wt(SYN, ss, sgp, st_)
            sd.inc(SYN.dma_start(wtile[:, :], src_flat_tiles[r]), 16)
            wt(SCA, sd)
            ss.inc(SCA.activation(wdummy[:, :], wtile[:, :], AF.Abs, accum_out=wabs[:, r:r + 1]))
        wt(VEC, ss)
        sv_.inc(VEC.reduce_sum(wabs[:, 32:33], wabs[:, 0:ntile], axis=AX.X))
        wt(TEN, sv_)
        st_.inc(TEN.matmul(psmall[0:1, 0:1], ones_col_f[:, :], wabs[:, 32:33], start=True, stop=True))
        wt(VEC, st_)
        VEC.tensor_scalar(wsc[:, 0:1], psmall[0:1, 0:1], 1.0 / n_elem, WQ_EPS, ALU.mult, ALU.max)
        vd()
        VEC.reciprocal(wsc[:, 1:2], wsc[:, 0:1])
        for _ in range(2):
            vd()
            VEC.tensor_tensor(wsc[:, 2:3], wsc[:, 0:1], wsc[:, 1:2], ALU.mult)
            vd()
            VEC.tensor_scalar(wsc[:, 2:3], wsc[:, 2:3], -1.0, 2.0, ALU.mult, ALU.add)
            vd()
            VEC.tensor_tensor(wsc[:, 1:2], wsc[:, 1:2], wsc[:, 2:3], ALU.mult)
        vd()
        sv_.inc(VEC.tensor_copy(wsc[:, 3:4], wsc[:, 1:2]))
        wt(TEN, sv_)
        TEN.matmul(psmall[:, 2:3], ones_row_f[:, :], wsc[:, 0:1], start=True, stop=True)
        st_.inc(TEN.matmul(psmall[:, 3:4], ones_row_f[:, :], wsc[:, 1:2], start=True, stop=True))
        wt(VEC, st_)
        VEC.tensor_copy(mhat_col, psmall[:, 2:3])
        sv_.inc(VEC.tensor_copy(wscale[:, :], psmall[:, 3:4]))

    def ternarize_store_rows(src_2d, in_chunks, out_cols, dst):
        OUTR = src_2d.shape[0]
        for r in range(OUTR // 128):
            wt(SYN, sgp, ss, sv_)
            sd.inc(SYN.dma_start(wtile[:, 0:out_cols], src_2d[r * 128:(r + 1) * 128, :]), 16)
            wt(VEC, sd, ss)
            vd()
            VEC.tensor_scalar(wtile[:, 0:out_cols], wtile[:, 0:out_cols], wscale[:, 0:1], None, ALU.mult)
            VEC.tensor_copy(i32s[:, 0:out_cols], wtile[:, 0:out_cols])
            sv_.inc(VEC.tensor_scalar(wtern[:, 0:out_cols], i32s[:, 0:out_cols], 1.0, -1.0,
                                      ALU.min, ALU.max))
            for g8 in range(in_chunks // 8):
                wt(TEN, sv_, ss)
                for i in range(8):
                    jc = g8 * 8 + i
                    mm = TEN.transpose(trB[:, i * 128:(i + 1) * 128],
                                       wtern[:, jc * 128:(jc + 1) * 128], idb[:, :])
                st_.inc(mm)
                wt(SCA, st_)
                ss.inc(SCA.copy(hq_fm[:, g8 * 1024:(g8 + 1) * 1024], trB[:, :]))
            wt(GPS, ss)
            dv = dst[:, :, r * 128:(r + 1) * 128]
            sgp.inc(GPS.dma_start(dv.rearrange("j p f -> p j f"),
                                  hq_fm[:, 0:in_chunks * 128].rearrange("p (j f) -> p j f", f=128)), 16)

    quant_matrix(fg_d.rearrange("(r p a) (b) -> r p (a b)", p=128, a=4), INNER * D, mhat[:, 0:1])
    ternarize_store_rows(fg_d, DC, D, wgR)
    quant_matrix(fu_d.rearrange("(r p a) (b) -> r p (a b)", p=128, a=4), INNER * D, mhat[:, 1:2])
    ternarize_store_rows(fu_d, DC, D, wuR)
    quant_matrix(fd_d.rearrange("(r p) (b) -> r p b", p=128), D * INNER, mhat[:, 2:3])
    ternarize_store_rows(fd_d, IC, INNER, wdR)

    def tcs(buf, i, t):
        return buf[:, i * T_TILE + t * 128:i * T_TILE + (t + 1) * 128]

    def fm_dst(buf, t):
        return buf.rearrange("p (i f) -> p i f", f=T_TILE)[:, :, t * 128:(t + 1) * 128]

    slab_rr = [0]
    slab_guard = [None] * NSLAB

    def load_slab(src_ap):
        i = slab_rr[0] % NSLAB
        slab_rr[0] += 1
        g = slab_guard[i]
        if g is not None:
            SYN.wait_ge(g[0].h, g[1])
        sd.inc(SYN.dma_start(slab[i][:, :], src_ap), 16)
        return i, sd.n

    for tile in range(cfg.N_TILES):
        seq = tile // cfg.TILES_PER_SEQ
        ti = tile % cfg.TILES_PER_SEQ
        first = (ti == 0)
        t0 = ti * T_TILE

        if tile > 0:
            SYN.wait_ge(sv_.h, sv_.n)
            SYN.wait_ge(sgp.h, sgp.n)
        xdma = []
        for t in range(NTC):
            sd.inc(SYN.dma_start(x_tok[t][:, :], x_d[seq, t0 + t * 128:t0 + (t + 1) * 128, :]), 16)
            xdma.append(sd.n)

        for t in range(NTC):
            SCA.wait_ge(sd.h, xdma[t])
            VEC.wait_ge(sd.h, xdma[t])
            wt(SCA, sv_)
            ss.inc(SCA.activation(tmp_tok[:, :], x_tok[t][:, :], AF.Square, accum_out=ist(t, 1)))
            VEC.reduce_sum(ist(t, 0), x_tok[t][:, :], axis=AX.X)
            wt(VEC, ss)
            vd()
            VEC.tensor_scalar(ist(t, 2), ist(t, 0), 1.0 / D, None, ALU.mult)
            VEC.tensor_scalar(ist(t, 14), ist(t, 1), 1.0 / D, None, ALU.mult)
            vd()
            VEC.tensor_tensor(ist(t, 15), ist(t, 2), ist(t, 2), ALU.mult)
            vd()
            VEC.tensor_tensor(ist(t, 14), ist(t, 14), ist(t, 15), ALU.subtract)
            sv_.inc(vd())
            wt(SCA, sv_)
            ss.inc(SCA.activation(ist(t, 15), ist(t, 14), AF.Sqrt, bias=cEPS))
            wt(VEC, ss)
            VEC.reciprocal(ist(t, 3), ist(t, 15))
            vd()
            sv_.inc(VEC.tensor_scalar(xn_tok[t][:, :], x_tok[t][:, :], ist(t, 2), ist(t, 3),
                                      ALU.subtract, ALU.mult))
            wt(SCA, sv_)
            ss.inc(SCA.activation(tmp_tok[:, :], xn_tok[t][:, :], AF.Square, accum_out=ist(t, 5)))
            VEC.reduce_sum(ist(t, 4), xn_tok[t][:, :], axis=AX.X)
            VEC.reduce_max(ist(t, 8), xn_tok[t][:, :], axis=AX.X, apply_absolute_value=True)
            wt(VEC, ss)
            vd()
            VEC.tensor_scalar(ist(t, 6), ist(t, 4), 1.0 / D, None, ALU.mult)
            VEC.tensor_scalar(ist(t, 14), ist(t, 5), 1.0 / D, None, ALU.mult)
            vd()
            VEC.tensor_tensor(ist(t, 15), ist(t, 6), ist(t, 6), ALU.mult)
            vd()
            VEC.tensor_tensor(ist(t, 14), ist(t, 14), ist(t, 15), ALU.subtract)
            sv_.inc(vd())
            wt(SCA, sv_)
            ss.inc(SCA.activation(ist(t, 15), ist(t, 14), AF.Sqrt, bias=cEPS))
            wt(VEC, ss)
            VEC.reciprocal(ist(t, 7), ist(t, 15))
            VEC.tensor_scalar(ist(t, 14), ist(t, 8), AQ_EPS, None, ALU.max)
            vd()
            VEC.reciprocal(ist(t, 15), ist(t, 14))
            vd()
            VEC.tensor_scalar(ist(t, 9), ist(t, 15), 127.0, None, ALU.mult)
            VEC.tensor_tensor(ist(t, 10), ist(t, 14), mhat[:, 0:1], ALU.mult)
            VEC.tensor_tensor(ist(t, 12), ist(t, 14), mhat[:, 1:2], ALU.mult)
            vd()
            VEC.tensor_scalar(ist(t, 10), ist(t, 10), 1.0 / 127.0, None, ALU.mult)
            VEC.tensor_scalar(ist(t, 12), ist(t, 12), 1.0 / 127.0, None, ALU.mult)
            sv_.inc(vd())
            VEC.tensor_scalar(tmp_tok[:, :], xn_tok[t][:, :], ist(t, 6), ist(t, 7),
                              ALU.subtract, ALU.mult)
            VEC.tensor_tensor(tmp_tok[:, :], tmp_tok[:, :], lngb[:, :], ALU.mult)
            sv_.inc(VEC.tensor_tensor(xn2_tok[t][:, :], tmp_tok[:, :], lnbb[:, :], ALU.add))
            VEC.tensor_scalar(tmp_tok[:, :], xn_tok[t][:, :], ist(t, 9), None, ALU.mult)
            VEC.tensor_copy(i32s[:, 0:D], tmp_tok[:, :])
            sv_.inc(VEC.tensor_single_scalar(xq_tok[t][:, :], i32s[:, 0:D], 0.0, ALU.add))

        for t in range(NTC):
            for src, dstfm in [(xn2_tok[t], xn2_fm), (xq_tok[t], xq_fm)]:
                wt(TEN, sv_, ss)
                for i in range(DC):
                    mm = TEN.transpose(trB[:, i * 128:(i + 1) * 128],
                                       src[:, i * 128:(i + 1) * 128], idb[:, :])
                st_.inc(mm)
                wt(SCA, st_)
                ss.inc(SCA.copy(fm_dst(dstfm, t), trB[:, :].rearrange("p (i f) -> p i f", f=128)))

        def dout_mm(base, c, bank):
            si, dn = load_slab(base[c])
            TEN.wait_ge(sd.h, dn)
            wt(TEN, sv_, ss)
            for j in range(DC):
                mm = TEN.matmul(PW[bank][:, 0:T_TILE], slab[si][:, j * 128:(j + 1) * 128],
                                xn2_fm[:, j * T_TILE:(j + 1) * T_TILE],
                                start=(j == 0), stop=(j == DC - 1))
            slab_guard[si] = (st_, st_.n + 1)
            st_.inc(mm)
            return PW[bank]

        for c in range(DC):
            pb = dout_mm(gateB, c, c % 4)
            wt(SCA, st_, sv_)
            ss.inc(SCA.activation(sg_t[:, 0:T_TILE], pb[:, 0:T_TILE], AF.Tanh,
                                  bias=biases[:, 16 + c:17 + c], scale=0.5))
            wt(VEC, ss)
            sv_.inc(VEC.tensor_scalar(gate_fm[:, c * T_TILE:(c + 1) * T_TILE],
                                      sg_t[:, 0:T_TILE], 0.5, 0.5, ALU.mult, ALU.add))
        for c in range(DC):
            pb = dout_mm(qkB, c, c % 4)
            wt(SCA, st_, sv_)
            ss.inc(SCA.activation(sg_t[:, 0:T_TILE], pb[:, 0:T_TILE], AF.Relu,
                                  bias=biases[:, 24 + c:25 + c], scale=-1.0))
            ss.inc(SCA.activation(sg_t[:, 256:512], sg_t[:, 0:T_TILE], AF.Exp, scale=-1.0))
            wt(VEC, ss)
            VEC.tensor_scalar(sg_t[:, 512:768], pb[:, 0:T_TILE], biases[:, c:c + 1], 0.0,
                              ALU.add, ALU.max)
            sv_.inc(VEC.tensor_tensor(q_fm[:, c * T_TILE:(c + 1) * T_TILE], sg_t[:, 512:768],
                                      sg_t[:, 256:512], ALU.add))
        for c in range(DC):
            pb = dout_mm(qkB, DC + c, c % 4)
            wt(VEC, st_, ss)
            sv_.inc(VEC.scalar_tensor_tensor(sg_t[:, 0:T_TILE], pb[:, 0:T_TILE],
                                             biases[:, 8 + c:9 + c],
                                             gate_fm[:, c * T_TILE:(c + 1) * T_TILE],
                                             ALU.add, ALU.mult))
            wt(SCA, sv_)
            ss.inc(SCA.activation(sg_t[:, 256:512], sg_t[:, 0:T_TILE], AF.Relu, scale=-1.0))
            ss.inc(SCA.activation(sg_t[:, 256:512], sg_t[:, 256:512], AF.Exp, scale=-1.0))
            wt(VEC, ss)
            VEC.tensor_single_scalar(sg_t[:, 512:768], sg_t[:, 0:T_TILE], 0.0, ALU.max)
            sv_.inc(VEC.tensor_tensor(k_fm[:, c * T_TILE:(c + 1) * T_TILE], sg_t[:, 512:768],
                                      sg_t[:, 256:512], ALU.add))

        vbank = {(0, 0): 2, (0, 1): 3, (1, 0): 4, (1, 1): 5}
        for j in range(DC):
            si, dn = load_slab(vR[j])
            TEN.wait_ge(sd.h, dn)
            if j == 0:
                wt(TEN, sv_, ss)
            for t in range(NTC):
                for h in range(2):
                    mm = TEN.matmul(PW[vbank[(t, h)]][:, :], tcs(xn2_fm, j, t),
                                    slab[si][:, h * 512:(h + 1) * 512],
                                    start=(j == 0), stop=(j == DC - 1))
            slab_guard[si] = (st_, st_.n + 1)
        st_.inc(mm)
        wt(VEC, st_)
        for t in range(NTC):
            VEC.tensor_tensor(vT[t][:, 0:512], PW[vbank[(t, 0)]][:, :], bvb[:, 0:512], ALU.add)
            sv_.inc(VEC.tensor_tensor(vT[t][:, 512:1024], PW[vbank[(t, 1)]][:, :],
                                      bvb[:, 512:1024], ALU.add))

        for t in range(NTC):
            wt(TEN, sv_, ss)
            for i in range(DC):
                mm = TEN.transpose(trB[:, i * 128:(i + 1) * 128], tcs(k_fm, i, t), idb[:, :])
            st_.inc(mm)
            wt(SCA, st_)
            ss.inc(SCA.copy(kT[t][:, :], trB[:, :]))

        for t in range(NTC):
            wt(TEN, ss, sv_)
            nblk = t + 1
            for b in range(nblk):
                for j in range(DC):
                    mm = TEN.matmul(PW[5][:, b * 128:(b + 1) * 128], tcs(k_fm, j, b),
                                    tcs(q_fm, j, t),
                                    start=(j == 0 and b == 0), stop=(j == DC - 1 and b == nblk - 1))
            st_.inc(mm)
            wt(VEC, st_)
            if t == 0:
                sv_.inc(VEC.tensor_tensor(sTm[:, 0:128], PW[5][:, 0:128], mask[:, :], ALU.mult))
            else:
                VEC.tensor_copy(sTm[:, 128:256], PW[5][:, 0:128])
                sv_.inc(VEC.tensor_tensor(sTm[:, 256:384], PW[5][:, 128:256], mask[:, :], ALU.mult))

        for t in range(NTC):
            wt(TEN, sv_, ss)
            started = False
            if not first:
                for j in range(DC):
                    lhs = tcs(q_fm, j, t)
                    TEN.matmul(PW[2][:, :], lhs, S_bf[:, j * D:j * D + 512], start=(j == 0), stop=False)
                    TEN.matmul(PW[3][:, :], lhs, S_bf[:, j * D + 512:(j + 1) * D], start=(j == 0), stop=False)
                    TEN.matmul(psmall[:, t:t + 1], lhs, sv_bf[:, j:j + 1], start=(j == 0), stop=False)
                started = True
            blks = [(0, 0)] if t == 0 else [(128, 0), (256, 1)]
            for bi, (off, b) in enumerate(blks):
                lhs = sTm[:, off:off + 128]
                fstart = (not started) and bi == 0
                last = bi == len(blks) - 1
                TEN.matmul(PW[2][:, :], lhs, vT[b][:, 0:512], start=fstart, stop=last)
                TEN.matmul(PW[3][:, :], lhs, vT[b][:, 512:1024], start=fstart, stop=last)
                mm = TEN.matmul(psmall[:, t:t + 1], lhs, ones_col_b[:, :], start=fstart, stop=last)
            st_.inc(mm)
            wt(VEC, st_)
            VEC.tensor_scalar(den_r[:, t:t + 1], psmall[:, t:t + 1], DEN_EPS, None, ALU.add)
            vd()
            VEC.reciprocal(den_r[:, t:t + 1], den_r[:, t:t + 1])
            vd()
            VEC.tensor_scalar(attn_tok[t][:, 0:512], PW[2][:, :], den_r[:, t:t + 1], None, ALU.mult)
            sv_.inc(VEC.tensor_scalar(attn_tok[t][:, 512:1024], PW[3][:, :], den_r[:, t:t + 1],
                                      None, ALU.mult))

        for t in range(NTC):
            wt(TEN, sv_, ss)
            for i in range(DC):
                mm = TEN.transpose(trB[:, i * 128:(i + 1) * 128],
                                   attn_tok[t][:, i * 128:(i + 1) * 128], idb[:, :])
            st_.inc(mm)
            wt(SCA, st_)
            ss.inc(SCA.copy(fm_dst(attn_fm, t), trB[:, :].rearrange("p (i f) -> p i f", f=128)))

        pbank = {(0, 0): 2, (0, 1): 3, (1, 0): 4, (1, 1): 5}
        for j in range(DC):
            si, dn = load_slab(projR[j])
            TEN.wait_ge(sd.h, dn)
            if j == 0:
                wt(TEN, ss, sv_)
            for t in range(NTC):
                for h in range(2):
                    mm = TEN.matmul(PW[pbank[(t, h)]][:, :], tcs(attn_fm, j, t),
                                    slab[si][:, h * 512:(h + 1) * 512],
                                    start=(j == 0), stop=(j == DC - 1))
            slab_guard[si] = (st_, st_.n + 1)
        st_.inc(mm)
        wt(VEC, st_)
        for t in range(NTC):
            for h in range(2):
                VEC.tensor_tensor(attnx_tok[t][:, h * 512:(h + 1) * 512], PW[pbank[(t, h)]][:, :],
                                  bpb[:, h * 512:(h + 1) * 512], ALU.add)
            sv_.inc(VEC.tensor_tensor(attnx_tok[t][:, :], attnx_tok[t][:, :], x_tok[t][:, :], ALU.add))

        for j in range(DC):
            wt(TEN, sv_, ss)
            for h in range(2):
                for t in range(NTC):
                    TEN.matmul(PW[h][:, :], kT[t][:, j * 128:(j + 1) * 128],
                               vT[t][:, h * 512:(h + 1) * 512],
                               start=(t == 0), stop=(t == NTC - 1))
            for t in range(NTC):
                mm = TEN.matmul(psmall[:, 8 + j:9 + j], kT[t][:, j * 128:(j + 1) * 128],
                                ones_col_b[:, :], start=(t == 0), stop=(t == NTC - 1))
            st_.inc(mm)
            wt(VEC, st_)
            if first:
                VEC.tensor_copy(S_st[:, j * D:j * D + 512], PW[0][:, :])
                VEC.tensor_copy(S_st[:, j * D + 512:(j + 1) * D], PW[1][:, :])
                sv_.inc(VEC.tensor_copy(sv_st[:, j:j + 1], psmall[:, 8 + j:9 + j]))
            else:
                VEC.tensor_tensor(S_st[:, j * D:j * D + 512], S_st[:, j * D:j * D + 512],
                                  PW[0][:, :], ALU.add)
                VEC.tensor_tensor(S_st[:, j * D + 512:(j + 1) * D],
                                  S_st[:, j * D + 512:(j + 1) * D], PW[1][:, :], ALU.add)
                sv_.inc(VEC.tensor_tensor(sv_st[:, j:j + 1], sv_st[:, j:j + 1],
                                          psmall[:, 8 + j:9 + j], ALU.add))
            wt(SCA, sv_)
            ss.inc(SCA.copy(S_bf[:, j * D:(j + 1) * D], S_st[:, j * D:(j + 1) * D]))
            ss.inc(SCA.copy(sv_bf[:, j:j + 1], sv_st[:, j:j + 1]))

        gb = {(0, 0): 2, (0, 1): 3, (1, 0): 4, (1, 1): 5}
        for osp in range(NOS // 2):
            gslabs = []
            for j in range(DC):
                sgi, dng = load_slab(wgR[j, :, osp * 1024:(osp + 1) * 1024])
                gslabs.append((sgi, dng))
            TEN.wait_ge(sd.h, gslabs[-1][1])
            wt(TEN, sv_, ss)
            for j in range(DC):
                for t in range(NTC):
                    lhs = tcs(xq_fm, j, t)
                    for osl in range(2):
                        mm = TEN.matmul(PW[gb[(osl, t)]][:, :], lhs,
                                        slab[gslabs[j][0]][:, osl * 512:(osl + 1) * 512],
                                        start=(j == 0), stop=(j == DC - 1))
            for j in range(DC):
                slab_guard[gslabs[j][0]] = (st_, st_.n + 1)
            st_.inc(mm)
            wt(SCA, st_, sv_)
            for osl in range(2):
                for t in range(NTC):
                    ss.inc(SCA.activation(sg_t[:, (osl * NTC + t) * 512:(osl * NTC + t + 1) * 512],
                                          PW[gb[(osl, t)]][:, :], AF.Silu, scale=ist(t, 10)))
            uslabs = []
            for j in range(DC):
                sui, dnu = load_slab(wuR[j, :, osp * 1024:(osp + 1) * 1024])
                uslabs.append((sui, dnu))
            TEN.wait_ge(sd.h, uslabs[-1][1])
            wt(TEN, ss)
            for j in range(DC):
                for t in range(NTC):
                    lhs = tcs(xq_fm, j, t)
                    for osl in range(2):
                        mm = TEN.matmul(PW[gb[(osl, t)]][:, :], lhs,
                                        slab[uslabs[j][0]][:, osl * 512:(osl + 1) * 512],
                                        start=(j == 0), stop=(j == DC - 1))
            for j in range(DC):
                slab_guard[uslabs[j][0]] = (st_, st_.n + 1)
            st_.inc(mm)
            wt(VEC, st_, ss)
            for osl in range(2):
                os_ = osp * 2 + osl
                for t in range(NTC):
                    sv_.inc(VEC.tensor_tensor(H_tok[t][:, os_ * 512:(os_ + 1) * 512],
                                              sg_t[:, (osl * NTC + t) * 512:(osl * NTC + t + 1) * 512],
                                              PW[gb[(osl, t)]][:, :], ALU.mult))

        for t in range(NTC):
            VEC.reduce_max(ist(t, 11), H_tok[t][:, :], axis=AX.X, apply_absolute_value=True)
            vd()
            VEC.tensor_tensor(ist(t, 14), ist(t, 11), ist(t, 12), ALU.mult)
            vd()
            VEC.tensor_scalar(ist(t, 14), ist(t, 14), AQ_EPS, None, ALU.max)
            vd()
            VEC.tensor_tensor(ist(t, 13), ist(t, 14), mhat[:, 2:3], ALU.mult)
            VEC.reciprocal(ist(t, 15), ist(t, 14))
            vd()
            VEC.tensor_scalar(ist(t, 13), ist(t, 13), 1.0 / 127.0, None, ALU.mult)
            VEC.tensor_tensor(ist(t, 15), ist(t, 15), ist(t, 12), ALU.mult)
            vd()
            VEC.tensor_scalar(ist(t, 15), ist(t, 15), 127.0, None, ALU.mult)
            sv_.inc(vd())
            VEC.tensor_scalar(H_tok[t][:, :], H_tok[t][:, :], ist(t, 15), None, ALU.mult)
            VEC.tensor_copy(i32s[:, :], H_tok[t][:, :])
            sv_.inc(VEC.tensor_single_scalar(hq_tok[t][:, :], i32s[:, :], 0.0, ALU.add))

        for t in range(NTC):
            for g8 in range(IC // 8):
                wt(TEN, sv_, ss)
                for i in range(8):
                    ic = g8 * 8 + i
                    mm = TEN.transpose(trB[:, i * 128:(i + 1) * 128],
                                       hq_tok[t][:, ic * 128:(ic + 1) * 128], idb[:, :])
                st_.inc(mm)
                wt(SCA, st_)
                ss.inc(SCA.copy(
                    hq_fm.rearrange("p (i f) -> p i f", f=T_TILE)[:, g8 * 8:(g8 + 1) * 8,
                                                                  t * 128:(t + 1) * 128],
                    trB[:, :].rearrange("p (i f) -> p i f", f=128)))

        dbank = {(0, 0): 2, (0, 1): 3, (1, 0): 4, (1, 1): 5}
        for i in range(IC):
            si, dn = load_slab(wdR[i])
            TEN.wait_ge(sd.h, dn)
            if i == 0:
                wt(TEN, ss, sv_)
            for t in range(NTC):
                lhs = tcs(hq_fm, i, t)
                for h in range(2):
                    mm = TEN.matmul(PW[dbank[(t, h)]][:, :], lhs,
                                    slab[si][:, h * 512:(h + 1) * 512],
                                    start=(i == 0), stop=(i == IC - 1))
            st_.inc(mm)
            slab_guard[si] = (st_, st_.n)
        wt(VEC, st_)
        for t in range(NTC):
            for h in range(2):
                VEC.tensor_scalar(out_tok[t][:, h * 512:(h + 1) * 512], PW[dbank[(t, h)]][:, :],
                                  ist(t, 13), None, ALU.mult)
            sv_.inc(VEC.tensor_tensor(out_tok[t][:, :], out_tok[t][:, :], attnx_tok[t][:, :], ALU.add))
            GPS.wait_ge(sv_.h, sv_.n)
            sgp.inc(GPS.dma_start(y_d[seq, t0 + t * 128:t0 + (t + 1) * 128, :], out_tok[t][:, :]), 16)

    GPS.wait_ge(sgp.h, sgp.n)
    return nc


def kernel(**inputs):
    cfg = Cfg()
    nc = build_program(cfg)
    n_cores = 8
    xfull = np.ascontiguousarray(np.asarray(inputs["L0_attn_x"]))
    in_maps = []
    for c in range(n_cores):
        m = {k: np.ascontiguousarray(np.asarray(v)) for k, v in inputs.items() if k != "L0_attn_x"}
        m["L0_attn_x"] = np.ascontiguousarray(xfull[c * cfg.B_local:(c + 1) * cfg.B_local])
        in_maps.append(m)
    res = run_bass_kernel_spmd(nc, in_maps, core_ids=list(range(n_cores)))
    out = np.concatenate([np.asarray(r["y"]) for r in res.results], axis=0)
    return out


# version 04
# speedup vs baseline: 1.2782x; 1.2782x over previous
import numpy as np
import ml_dtypes
import concourse.bass as bass
import concourse.mybir as mybir
from concourse.bass_utils import run_bass_kernel_spmd

F32 = mybir.dt.float32
BF16 = mybir.dt.bfloat16
I32 = mybir.dt.int32
AF = mybir.ActivationFunctionType
ALU = mybir.AluOpType
AX = mybir.AxisListType

LN_EPS = 1e-5
DEN_EPS = 1e-6
WQ_EPS = 1e-5
AQ_EPS = 1e-5


class Cfg:
    def __init__(self, B_local=4, T_seq=2048, D=1024, INNER=4096):
        self.B_local = B_local
        self.T_seq = T_seq
        self.D = D
        self.INNER = INNER
        self.DC = D // 128
        self.IC = INNER // 128
        self.T_TILE = 256
        self.NTC = self.T_TILE // 128
        self.TILES_PER_SEQ = T_seq // self.T_TILE
        self.N_TILES = self.B_local * self.TILES_PER_SEQ


def build_program(cfg: Cfg):
    from contextlib import ExitStack
    nc = bass.Bass("TRN2", target_bir_lowering=False)
    stack = ExitStack()
    D, DC, IC, INNER = cfg.D, cfg.DC, cfg.IC, cfg.INNER
    NTC, T_TILE = cfg.NTC, cfg.T_TILE
    NOS = INNER // 512

    x_d = nc.dram_tensor("L0_attn_x", [cfg.B_local, cfg.T_seq, D], F32, kind="ExternalInput").ap()
    y_d = nc.dram_tensor("y", [cfg.B_local, cfg.T_seq, D], F32, kind="ExternalOutput").ap()
    wqkv_d = nc.dram_tensor("L0_attn_w_qkv", [D, 3 * D], F32, kind="ExternalInput").ap()
    bqkv_d = nc.dram_tensor("L0_attn_b_qkv", [3 * D], F32, kind="ExternalInput").ap()
    wgate_d = nc.dram_tensor("L0_attn_w_gate", [D, D], F32, kind="ExternalInput").ap()
    bgate_d = nc.dram_tensor("L0_attn_b_gate", [D], F32, kind="ExternalInput").ap()
    wproj_d = nc.dram_tensor("L0_attn_w_proj", [D, D], F32, kind="ExternalInput").ap()
    bproj_d = nc.dram_tensor("L0_attn_b_proj", [D], F32, kind="ExternalInput").ap()
    lng_d = nc.dram_tensor("L0_attn_ln_g", [D], F32, kind="ExternalInput").ap()
    lnb_d = nc.dram_tensor("L0_attn_ln_b", [D], F32, kind="ExternalInput").ap()
    fg_d = nc.dram_tensor("L0_ffn_w_gate", [INNER, D], F32, kind="ExternalInput").ap()
    fu_d = nc.dram_tensor("L0_ffn_w_up", [INNER, D], F32, kind="ExternalInput").ap()
    fd_d = nc.dram_tensor("L0_ffn_w_down", [D, INNER], F32, kind="ExternalInput").ap()

    qkB = nc.dram_tensor("qkB", [2 * DC, 128, D], BF16, kind="Internal").ap()
    gateB = nc.dram_tensor("gateB", [DC, 128, D], BF16, kind="Internal").ap()
    vR = nc.dram_tensor("vR", [DC, 128, D], BF16, kind="Internal").ap()
    projR = nc.dram_tensor("projR", [DC, 128, D], BF16, kind="Internal").ap()
    wgR = nc.dram_tensor("wgR", [DC, 128, INNER], BF16, kind="Internal").ap()
    wuR = nc.dram_tensor("wuR", [DC, 128, INNER], BF16, kind="Internal").ap()
    wdR = nc.dram_tensor("wdR", [IC, 128, D], BF16, kind="Internal").ap()

    mask_np = np.triu(np.ones((128, 128), np.float32)).astype(ml_dtypes.bfloat16)
    mask_d = nc.inline_tensor(mask_np, name="cmask_c").ap()
    idb_d = nc.inline_tensor(np.eye(128, dtype=np.float32).astype(ml_dtypes.bfloat16), name="idb_c").ap()
    ones_col_b_d = nc.inline_tensor(np.ones((128, 1), np.float32).astype(ml_dtypes.bfloat16), name="onescb").ap()
    ones_row_f_d = nc.inline_tensor(np.ones((1, 128), np.float32), name="onesrf").ap()
    ones_col_f_d = nc.inline_tensor(np.ones((128, 1), np.float32), name="onescf").ap()
    CONST_VALS = [LN_EPS, DEN_EPS, -1.0, 0.5, WQ_EPS, 1.0, 0.0, 127.0]
    consts_np = np.repeat(np.array([CONST_VALS], np.float32), 128, axis=0)
    consts_d = nc.inline_tensor(consts_np, name="consts_c").ap()

    ctx = stack.enter_context
    A = nc.alloc_sbuf_tensor

    consts = A("consts_s", [128, 8], F32)
    mask = A("mask", [128, 128], BF16)
    idb = A("idb_s", [128, 128], BF16)
    ones_col_b = A("ones_col_b", [128, 1], BF16)
    ones_row_f = A("ones_row_f", [1, 128], F32)
    ones_col_f = A("ones_col_f", [128, 1], F32)
    biases = A("biases", [128, 40], F32)
    bq_cols = A("bq_cols", [128, 3 * DC], F32)
    bg_cols = A("bg_cols", [128, DC], F32)
    lngb = A("lngb", [128, D], BF16)
    lnbb = A("lnbb", [128, D], BF16)
    bvb = A("bvb", [128, D], BF16)
    bpb = A("bpb", [128, D], BF16)
    mhat = A("mhat", [128, 3], F32)
    wscale = A("wscale", [128, 1], F32)
    wabs = A("wabs", [128, 33], F32)
    wsc = A("wsc", [1, 8], F32)
    stats = A("stats", [128, 64], F32)
    den_r = A("den_r", [128, 2], F32)
    _par = [0]
    ist = lambda t, c: stats[:, 32 * _par[0] + 16 * t + c:32 * _par[0] + 16 * t + c + 1]

    x_tok = [A(f"x_tok{t}", [128, D], F32) for t in range(NTC)]
    xn2_tok = [A(f"xn2_tok{t}", [128, D], BF16) for t in range(NTC)]
    xq_tok = [A(f"xq_tok{t}", [128, D], BF16) for t in range(NTC)]
    attnx_tok = [A(f"attnx_tok{t}", [128, D], F32) for t in range(NTC)]
    out_tok = attnx_tok

    fmA = A("fmA", [128, 8192], BF16)
    xn2_fm = fmA[:, 0:2048]
    gate_fm = fmA[:, 2048:4096]
    q_fm = fmA[:, 4096:6144]
    attn_fm = fmA[:, 6144:8192]
    hq_fm = fmA[:, 0:8192]
    wtern = fmA[:, 4096:8192]
    xq_fm = A("xq_fm", [128, 2048], BF16)
    k_fm = A("k_fm", [128, 2048], BF16)
    attn_tok = [k_fm[:, 0:1024], k_fm[:, 1024:2048]]
    kT = [A(f"kT{t}", [128, D], BF16) for t in range(NTC)]
    vT = [A(f"vT{t}", [128, D], BF16) for t in range(NTC)]

    S_st = A("S_st", [128, DC * D], F32)
    S_bf = A("S_bf", [128, DC * D], BF16)
    sv_st = A("sv_st", [128, DC], F32)
    sv_bf = A("sv_bf", [128, DC], BF16)

    H_tok = [A(f"H_tok{t}", [128, INNER], F32) for t in range(NTC)]
    xn_tok = [H_tok[t][:, D:2 * D] for t in range(NTC)]
    i32s = A("i32s", [128, 4096], I32)
    hq_tok = [A(f"hq_tok{t}", [128, INNER], BF16) for t in range(NTC)]
    sg_t = A("sg_t", [128, 2048], F32)
    tmp_tok = sg_t[:, 0:D]
    wtile = H_tok[0]
    wdummy = H_tok[1]
    rowv = H_tok[0][0:1, :]
    sTm = xq_tok[0][:, 0:384]

    NSLAB = 9
    slab = [A(f"slab{i}", [128, 1024], BF16) for i in range(NSLAB)]

    trBs = [ctx(nc.psum_tensor("trBa", [128, 1024], BF16)),
            ctx(nc.psum_tensor("trBb", [128, 1024], BF16))]
    PW = [ctx(nc.psum_tensor(f"pw{i}", [128, 512], F32)) for i in range(5)]
    psmall = ctx(nc.psum_tensor("psmall", [128, 512], F32))

    class Sem:
        def __init__(self, name):
            self.h = stack.enter_context(nc.semaphore(name))
            self.n = 0

        def inc(self, instr, k=1):
            instr.then_inc(self.h, k)
            self.n += k
            return instr

    sd = Sem("sd")
    sgp = Sem("sgp")
    sv_ = Sem("sv")
    ss = Sem("ss")
    st_ = Sem("st")

    TEN, VEC, SCA, SYN, GPS = nc.tensor, nc.vector, nc.scalar, nc.sync, nc.gpsimd

    for i, v in enumerate(CONST_VALS):
        nc.const_aps.register(F32, v, consts[:, i:i + 1])
    cEPS = consts[:, 0:1]

    def wt(eng, *sems):
        for s in sems:
            if s.n > 0:
                eng.wait_ge(s.h, s.n)

    def vd():
        return VEC.drain()

    for dst, src in [(consts, consts_d), (mask, mask_d), (idb, idb_d),
                     (ones_col_b, ones_col_b_d), (ones_row_f, ones_row_f_d),
                     (ones_col_f, ones_col_f_d)]:
        sd.inc(SYN.dma_start(dst[:, :], src), 16)
    with nc.allow_non_contiguous_dma("tiny bias transposes"):
        sd.inc(SYN.dma_start(bq_cols[:, :], bqkv_d.rearrange("(c p) -> p c", p=128)), 16)
        sd.inc(SYN.dma_start(bg_cols[:, :], bgate_d.rearrange("(c p) -> p c", p=128)), 16)
    sd.inc(SYN.dma_start(rowv[:, 0:D], lng_d.rearrange("(o d) -> o d", o=1)), 16)
    sd.inc(SYN.dma_start(rowv[:, D:2 * D], lnb_d.rearrange("(o d) -> o d", o=1)), 16)
    sd.inc(SYN.dma_start(rowv[:, 2 * D:3 * D], bqkv_d[2 * D:3 * D].rearrange("(o d) -> o d", o=1)), 16)
    sd.inc(SYN.dma_start(rowv[:, 3 * D:4 * D], bproj_d.rearrange("(o d) -> o d", o=1)), 16)

    wt(VEC, sd)
    VEC.tensor_copy(biases[:, 0:DC], bq_cols[:, 0:DC])
    VEC.tensor_copy(biases[:, 8:8 + DC], bq_cols[:, DC:2 * DC])
    VEC.tensor_scalar(biases[:, 16:16 + DC], bg_cols[:, :], 0.5, None, ALU.mult)
    sv_.inc(VEC.tensor_scalar(biases[:, 24:24 + DC], bq_cols[:, 0:DC], -1.0, None, ALU.mult))

    def bcast_row(dst, col0):
        wt(TEN, sd, sv_, ss)
        TEN.matmul(PW[0][:, :], ones_row_f[:, :], rowv[:, col0:col0 + 512], start=True, stop=True)
        st_.inc(TEN.matmul(PW[1][:, :], ones_row_f[:, :], rowv[:, col0 + 512:col0 + 1024],
                           start=True, stop=True))
        wt(VEC, st_)
        VEC.tensor_copy(dst[:, 0:512], PW[0][:, :])
        sv_.inc(VEC.tensor_copy(dst[:, 512:1024], PW[1][:, :]))

    bcast_row(lngb, 0)
    bcast_row(lnbb, D)
    bcast_row(bvb, 2 * D)
    bcast_row(bpb, 3 * D)

    def conv_blocks(src_ap, in_chunks, out_cols, dst):
        for j in range(in_chunks):
            wt(SYN, sgp, st_, ss)
            sd.inc(SYN.dma_start(wtile[:, 0:out_cols], src_ap[j * 128:(j + 1) * 128, :]), 16)
            wt(VEC, sd, ss)
            sv_.inc(VEC.tensor_copy(wtern[:, 0:out_cols], wtile[:, 0:out_cols]))
            wt(GPS, sv_)
            dv = dst[:, :, j * 128:(j + 1) * 128]
            sgp.inc(GPS.dma_start(dv.rearrange("c p f -> p c f"),
                                  wtern[:, 0:out_cols].rearrange("p (c f) -> p c f", f=128)), 16)

    def conv_rows(src_ap, in_chunks, out_cols, dst):
        for j in range(in_chunks):
            wt(SYN, sgp, st_, ss)
            sd.inc(SYN.dma_start(wtile[:, 0:out_cols], src_ap[j * 128:(j + 1) * 128, :]), 16)
            wt(VEC, sd, ss)
            sv_.inc(VEC.tensor_copy(wtern[:, 0:out_cols], wtile[:, 0:out_cols]))
            wt(GPS, sv_)
            sgp.inc(GPS.dma_start(dst[j], wtern[:, 0:out_cols]), 16)

    conv_blocks(wqkv_d[:, 0:2 * D], DC, 2 * D, qkB)
    conv_blocks(wgate_d, DC, D, gateB)
    conv_rows(wqkv_d[:, 2 * D:3 * D], DC, D, vR)
    conv_rows(wproj_d, DC, D, projR)

    def quant_matrix(src_flat_tiles, n_elem, mhat_col):
        ntile = n_elem // (128 * 4096)
        for r in range(ntile):
            wt(SYN, ss, sgp, st_)
            sd.inc(SYN.dma_start(wtile[:, :], src_flat_tiles[r]), 16)
            wt(SCA, sd)
            ss.inc(SCA.activation(wdummy[:, :], wtile[:, :], AF.Abs, accum_out=wabs[:, r:r + 1]))
        wt(VEC, ss)
        sv_.inc(VEC.reduce_sum(wabs[:, 32:33], wabs[:, 0:ntile], axis=AX.X))
        wt(TEN, sv_)
        st_.inc(TEN.matmul(psmall[0:1, 0:1], ones_col_f[:, :], wabs[:, 32:33], start=True, stop=True))
        wt(VEC, st_)
        VEC.tensor_scalar(wsc[:, 0:1], psmall[0:1, 0:1], 1.0 / n_elem, WQ_EPS, ALU.mult, ALU.max)
        vd()
        VEC.reciprocal(wsc[:, 1:2], wsc[:, 0:1])
        for _ in range(2):
            vd()
            VEC.tensor_tensor(wsc[:, 2:3], wsc[:, 0:1], wsc[:, 1:2], ALU.mult)
            vd()
            VEC.tensor_scalar(wsc[:, 2:3], wsc[:, 2:3], -1.0, 2.0, ALU.mult, ALU.add)
            vd()
            VEC.tensor_tensor(wsc[:, 1:2], wsc[:, 1:2], wsc[:, 2:3], ALU.mult)
        vd()
        sv_.inc(VEC.tensor_copy(wsc[:, 3:4], wsc[:, 1:2]))
        wt(TEN, sv_)
        TEN.matmul(psmall[:, 2:3], ones_row_f[:, :], wsc[:, 0:1], start=True, stop=True)
        st_.inc(TEN.matmul(psmall[:, 3:4], ones_row_f[:, :], wsc[:, 1:2], start=True, stop=True))
        wt(VEC, st_)
        VEC.tensor_copy(mhat_col, psmall[:, 2:3])
        sv_.inc(VEC.tensor_copy(wscale[:, :], psmall[:, 3:4]))

    def ternarize_store_rows(src_2d, in_chunks, out_cols, dst):
        OUTR = src_2d.shape[0]
        for r in range(OUTR // 128):
            wt(SYN, sgp, ss, sv_)
            sd.inc(SYN.dma_start(wtile[:, 0:out_cols], src_2d[r * 128:(r + 1) * 128, :]), 16)
            wt(VEC, sd, ss)
            vd()
            VEC.tensor_scalar(wtile[:, 0:out_cols], wtile[:, 0:out_cols], wscale[:, 0:1], None, ALU.mult)
            VEC.tensor_copy(i32s[:, 0:out_cols], wtile[:, 0:out_cols])
            sv_.inc(VEC.tensor_scalar(wtern[:, 0:out_cols], i32s[:, 0:out_cols], 1.0, -1.0,
                                      ALU.min, ALU.max))
            for g8 in range(in_chunks // 8):
                wt(TEN, sv_, ss)
                for i in range(8):
                    jc = g8 * 8 + i
                    mm = TEN.transpose(trBs[0][:, i * 128:(i + 1) * 128],
                                       wtern[:, jc * 128:(jc + 1) * 128], idb[:, :])
                st_.inc(mm)
                wt(SCA, st_)
                ss.inc(SCA.copy(hq_fm[:, g8 * 1024:(g8 + 1) * 1024], trBs[0][:, :]))
            wt(GPS, ss)
            dv = dst[:, :, r * 128:(r + 1) * 128]
            sgp.inc(GPS.dma_start(dv.rearrange("j p f -> p j f"),
                                  hq_fm[:, 0:in_chunks * 128].rearrange("p (j f) -> p j f", f=128)), 16)

    quant_matrix(fg_d.rearrange("(r p a) (b) -> r p (a b)", p=128, a=4), INNER * D, mhat[:, 0:1])
    ternarize_store_rows(fg_d, DC, D, wgR)
    quant_matrix(fu_d.rearrange("(r p a) (b) -> r p (a b)", p=128, a=4), INNER * D, mhat[:, 1:2])
    ternarize_store_rows(fu_d, DC, D, wuR)
    quant_matrix(fd_d.rearrange("(r p) (b) -> r p b", p=128), D * INNER, mhat[:, 2:3])
    ternarize_store_rows(fd_d, IC, INNER, wdR)

    def tcs(buf, i, t):
        return buf[:, i * T_TILE + t * 128:i * T_TILE + (t + 1) * 128]

    def fm_dst(buf, t):
        return buf.rearrange("p (i f) -> p i f", f=T_TILE)[:, :, t * 128:(t + 1) * 128]

    guards = {}
    waited = {}

    def gw(eng, key):
        g = guards.get(key)
        if g is None:
            return
        wkey = (id(eng), id(g[0]))
        if waited.get(wkey, -1) >= g[1]:
            return
        eng.wait_ge(g[0].h, g[1])
        waited[wkey] = g[1]

    def gset(key, sem):
        guards[key] = (sem, sem.n)

    def wt2(eng, *sems):
        for s in sems:
            if s.n > 0:
                wkey = (id(eng), id(s))
                if waited.get(wkey, -1) >= s.n:
                    continue
                eng.wait_ge(s.h, s.n)
                waited[wkey] = s.n

    slab_rr = [0]
    slab_guard = [None] * NSLAB

    def load_slab(src_ap):
        i = slab_rr[0] % NSLAB
        slab_rr[0] += 1
        g = slab_guard[i]
        if g is not None:
            wkey = (id(SYN), id(g[0]), "slab", i)
            SYN.wait_ge(g[0].h, g[1])
        sd.inc(SYN.dma_start(slab[i][:, :], src_ap), 16)
        return i, sd.n

    tr_half = [0]

    def emit_transposes(srcs, dst_fn, evac_sem=None):
        n = len(srcs)
        for base in range(0, n, 8):
            nblk = min(8, n - base)
            h = tr_half[0] % 2
            tr_half[0] += 1
            bank = trBs[h]
            gw(TEN, f"trB{h}")
            for bi in range(nblk):
                mm = TEN.transpose(bank[:, bi * 128:(bi + 1) * 128], srcs[base + bi], idb[:, :])
            st_.inc(mm)
            wt2(SCA, st_)
            sem = evac_sem or ss
            sem.inc(SCA.copy(dst_fn(base, nblk),
                             bank[:, 0:nblk * 128].rearrange("p (i f) -> p i f", f=128)))
            gset(f"trB{h}", sem)

    def ph1_emit(tile):
        seq = tile // cfg.TILES_PER_SEQ
        t0 = (tile % cfg.TILES_PER_SEQ) * T_TILE
        gw(SYN, "x_free")
        xd = []
        for t in range(NTC):
            sd.inc(SYN.dma_start(x_tok[t][:, :], x_d[seq, t0 + t * 128:t0 + (t + 1) * 128, :]), 16)
            xd.append(sd.n)
        guards["xdma"] = (sd, sd.n)
        return xd

    def ph2_emit(tile, xd):
        wt2(VEC, st_)
        wt2(SCA, st_)
        gw(SCA, "sgtA")
        gw(SCA, "sgtB")
        for t in range(NTC):
            SCA.wait_ge(sd.h, xd[t])
            VEC.wait_ge(sd.h, xd[t])
            wt2(SCA, sv_)
            ss.inc(SCA.activation(tmp_tok[:, :], x_tok[t][:, :], AF.Square, accum_out=ist(t, 1)))
            VEC.reduce_sum(ist(t, 0), x_tok[t][:, :], axis=AX.X)
            wt2(VEC, ss)
            vd()
            VEC.tensor_scalar(ist(t, 2), ist(t, 0), 1.0 / D, None, ALU.mult)
            VEC.tensor_scalar(ist(t, 14), ist(t, 1), 1.0 / D, None, ALU.mult)
            vd()
            VEC.tensor_tensor(ist(t, 15), ist(t, 2), ist(t, 2), ALU.mult)
            vd()
            VEC.tensor_tensor(ist(t, 14), ist(t, 14), ist(t, 15), ALU.subtract)
            sv_.inc(vd())
            wt2(SCA, sv_)
            ss.inc(SCA.activation(ist(t, 15), ist(t, 14), AF.Sqrt, bias=cEPS))
            wt2(VEC, ss)
            VEC.reciprocal(ist(t, 3), ist(t, 15))
            vd()
            sv_.inc(VEC.tensor_scalar(xn_tok[t][:, :], x_tok[t][:, :], ist(t, 2), ist(t, 3),
                                      ALU.subtract, ALU.mult))
            wt2(SCA, sv_)
            ss.inc(SCA.activation(tmp_tok[:, :], xn_tok[t][:, :], AF.Square, accum_out=ist(t, 5)))
            VEC.reduce_sum(ist(t, 4), xn_tok[t][:, :], axis=AX.X)
            VEC.reduce_max(ist(t, 8), xn_tok[t][:, :], axis=AX.X, apply_absolute_value=True)
            wt2(VEC, ss)
            vd()
            VEC.tensor_scalar(ist(t, 6), ist(t, 4), 1.0 / D, None, ALU.mult)
            VEC.tensor_scalar(ist(t, 14), ist(t, 5), 1.0 / D, None, ALU.mult)
            vd()
            VEC.tensor_tensor(ist(t, 15), ist(t, 6), ist(t, 6), ALU.mult)
            vd()
            VEC.tensor_tensor(ist(t, 14), ist(t, 14), ist(t, 15), ALU.subtract)
            sv_.inc(vd())
            wt2(SCA, sv_)
            ss.inc(SCA.activation(ist(t, 15), ist(t, 14), AF.Sqrt, bias=cEPS))
            wt2(VEC, ss)
            VEC.reciprocal(ist(t, 7), ist(t, 15))
            VEC.tensor_scalar(ist(t, 14), ist(t, 8), AQ_EPS, None, ALU.max)
            vd()
            VEC.reciprocal(ist(t, 15), ist(t, 14))
            vd()
            VEC.tensor_scalar(ist(t, 9), ist(t, 15), 127.0, None, ALU.mult)
            VEC.tensor_tensor(ist(t, 10), ist(t, 14), mhat[:, 0:1], ALU.mult)
            VEC.tensor_tensor(ist(t, 12), ist(t, 14), mhat[:, 1:2], ALU.mult)
            vd()
            VEC.tensor_scalar(ist(t, 10), ist(t, 10), 1.0 / 127.0, None, ALU.mult)
            VEC.tensor_scalar(ist(t, 12), ist(t, 12), 1.0 / 127.0, None, ALU.mult)
            sv_.inc(vd())
            VEC.tensor_scalar(tmp_tok[:, :], xn_tok[t][:, :], ist(t, 6), ist(t, 7),
                              ALU.subtract, ALU.mult)
            VEC.tensor_tensor(tmp_tok[:, :], tmp_tok[:, :], lngb[:, :], ALU.mult)
            sv_.inc(VEC.tensor_tensor(xn2_tok[t][:, :], tmp_tok[:, :], lnbb[:, :], ALU.add))
            VEC.tensor_scalar(tmp_tok[:, :], xn_tok[t][:, :], ist(t, 9), None, ALU.mult)
            VEC.tensor_copy(i32s[:, 0:D], tmp_tok[:, :])
            sv_.inc(VEC.tensor_single_scalar(xq_tok[t][:, :], i32s[:, 0:D], 0.0, ALU.add))
        guards["ph2done"] = (sv_, sv_.n)
        guards["stats" + str(_par[0])] = (sv_, sv_.n)
        gset("sgtA", sv_)
        gset("sgtB", sv_)

    _par[0] = 0
    xd0 = ph1_emit(0)
    ph2_emit(0, xd0)
    next_xd = xd0

    for tile in range(cfg.N_TILES):
        seq = tile // cfg.TILES_PER_SEQ
        ti = tile % cfg.TILES_PER_SEQ
        first = (ti == 0)
        t0 = ti * T_TILE
        _par[0] = tile % 2

        gw(TEN, "ph2done")
        gw(TEN, "xn2_fm")
        gw(TEN, "xq_fm")
        for t in range(NTC):
            for src_tok, dstfm in [(xn2_tok[t], xn2_fm), (xq_tok[t], xq_fm)]:
                emit_transposes(
                    [src_tok[:, i * 128:(i + 1) * 128] for i in range(DC)],
                    lambda base, nblk, dstfm=dstfm, t=t: dstfm.rearrange(
                        "p (i f) -> p i f", f=T_TILE)[:, base:base + nblk,
                                                      t * 128:(t + 1) * 128])
        guards["xn2_fm_rdy"] = (ss, ss.n)
        guards["xq_fm_rdy"] = (ss, ss.n)

        def dout_mm(base, c, bank):
            si, dn = load_slab(base[c])
            TEN.wait_ge(sd.h, dn)
            gw(TEN, f"pw{bank}")
            gw(TEN, "xn2_fm_rdy")
            for j in range(DC):
                mm = TEN.matmul(PW[bank][:, 0:T_TILE], slab[si][:, j * 128:(j + 1) * 128],
                                xn2_fm[:, j * T_TILE:(j + 1) * T_TILE],
                                start=(j == 0), stop=(j == DC - 1))
            st_.inc(mm)
            slab_guard[si] = (st_, st_.n)
            return PW[bank]

        for c in range(DC):
            pb = dout_mm(gateB, c, c % 4)
            wt2(SCA, st_)
            gw(SCA, "sgtC")
            ss.inc(SCA.activation(sg_t[:, 1536:1536 + T_TILE], pb[:, 0:T_TILE], AF.Tanh,
                                  bias=biases[:, 16 + c:17 + c], scale=0.5))
            wt2(VEC, ss)
            sv_.inc(VEC.tensor_scalar(gate_fm[:, c * T_TILE:(c + 1) * T_TILE],
                                      sg_t[:, 1536:1536 + T_TILE], 0.5, 0.5, ALU.mult, ALU.add))
            gset(f"pw{c % 4}", sv_)
            gset("sgtC", sv_)
        for c in range(DC):
            pb = dout_mm(qkB, c, c % 4)
            wt2(SCA, st_)
            rk = "sgtA" if c % 2 == 0 else "sgtB"
            gw(SCA, rk)
            if c == 0:
                gw(VEC, "q_fm")
            soff = (c % 2) * 768
            ss.inc(SCA.activation(sg_t[:, soff:soff + 256], pb[:, 0:T_TILE], AF.Relu,
                                  bias=biases[:, 24 + c:25 + c], scale=-1.0))
            ss.inc(SCA.activation(sg_t[:, soff + 256:soff + 512], sg_t[:, soff:soff + 256],
                                  AF.Exp, scale=-1.0))
            wt2(VEC, ss)
            VEC.tensor_scalar(sg_t[:, soff + 512:soff + 768], pb[:, 0:T_TILE],
                              biases[:, c:c + 1], 0.0, ALU.add, ALU.max)
            sv_.inc(VEC.tensor_tensor(q_fm[:, c * T_TILE:(c + 1) * T_TILE],
                                      sg_t[:, soff + 512:soff + 768],
                                      sg_t[:, soff + 256:soff + 512], ALU.add))
            gset(f"pw{c % 4}", sv_)
            gset(rk, sv_)
        guards["q_fm_rdy"] = (sv_, sv_.n)
        for c in range(DC):
            pb = dout_mm(qkB, DC + c, c % 4)
            wt2(VEC, st_)
            soff = (c % 2) * 768
            rk = "sgtA" if c % 2 == 0 else "sgtB"
            if c == 0:
                gw(VEC, "attn_tok")
                gw(VEC, "k_fm")
            gw(VEC, rk)
            sv_.inc(VEC.scalar_tensor_tensor(sg_t[:, soff:soff + 256], pb[:, 0:T_TILE],
                                             biases[:, 8 + c:9 + c],
                                             gate_fm[:, c * T_TILE:(c + 1) * T_TILE],
                                             ALU.add, ALU.mult))
            gset(f"pw{c % 4}", sv_)
            wt2(SCA, sv_)
            ss.inc(SCA.activation(sg_t[:, soff + 256:soff + 512], sg_t[:, soff:soff + 256],
                                  AF.Relu, scale=-1.0))
            ss.inc(SCA.activation(sg_t[:, soff + 256:soff + 512], sg_t[:, soff + 256:soff + 512],
                                  AF.Exp, scale=-1.0))
            wt2(VEC, ss)
            VEC.tensor_single_scalar(sg_t[:, soff + 512:soff + 768], sg_t[:, soff:soff + 256],
                                     0.0, ALU.max)
            sv_.inc(VEC.tensor_tensor(k_fm[:, c * T_TILE:(c + 1) * T_TILE],
                                      sg_t[:, soff + 512:soff + 768],
                                      sg_t[:, soff + 256:soff + 512], ALU.add))
            gset(rk, sv_)
        guards["k_fm_rdy"] = (sv_, sv_.n)

        vbank = {(0, 0): 1, (0, 1): 2, (1, 0): 3, (1, 1): 4}
        for j in range(DC):
            si, dn = load_slab(vR[j])
            TEN.wait_ge(sd.h, dn)
            if j == 0:
                for b in range(1, 5):
                    gw(TEN, f"pw{b}")
                gw(TEN, "xn2_fm_rdy")
            for t in range(NTC):
                for h in range(2):
                    mm = TEN.matmul(PW[vbank[(t, h)]][:, :], tcs(xn2_fm, j, t),
                                    slab[si][:, h * 512:(h + 1) * 512],
                                    start=(j == 0), stop=(j == DC - 1))
            if j == DC - 1:
                st_.inc(mm)
            slab_guard[si] = (st_, st_.n + (1 if j < DC - 1 else 0))
        guards["xn2_fm"] = (st_, st_.n)
        wt2(VEC, st_)
        for t in range(NTC):
            gw(VEC, "vT")
            VEC.tensor_tensor(vT[t][:, 0:512], PW[vbank[(t, 0)]][:, :], bvb[:, 0:512], ALU.add)
            sv_.inc(VEC.tensor_tensor(vT[t][:, 512:1024], PW[vbank[(t, 1)]][:, :],
                                      bvb[:, 512:1024], ALU.add))
        for b in range(1, 5):
            gset(f"pw{b}", sv_)
        guards["vT_rdy"] = (sv_, sv_.n)

        gw(TEN, "k_fm_rdy")
        gw(TEN, "kT_free")
        for t in range(NTC):
            emit_transposes(
                [tcs(k_fm, i, t) for i in range(DC)],
                lambda base, nblk, t=t: kT[t][:, base * 128:(base + nblk) * 128])
        guards["kT_rdy"] = (ss, ss.n)
        guards["k_fm"] = (st_, st_.n)

        for t in range(NTC):
            gw(TEN, "q_fm_rdy")
            gw(TEN, "k_fm_rdy")
            gw(TEN, "psmall")
            nblk = t + 1
            for b in range(nblk):
                for j in range(DC):
                    mm = TEN.matmul(psmall[:, 256 + b * 128:256 + (b + 1) * 128], tcs(k_fm, j, b),
                                    tcs(q_fm, j, t),
                                    start=(j == 0 and b == 0), stop=(j == DC - 1 and b == nblk - 1))
            st_.inc(mm)
            wt2(VEC, st_)
            if t == 0:
                sv_.inc(VEC.tensor_tensor(sTm[:, 0:128], psmall[:, 256:384], mask[:, :], ALU.mult))
            else:
                VEC.tensor_copy(sTm[:, 128:256], psmall[:, 256:384])
                sv_.inc(VEC.tensor_tensor(sTm[:, 256:384], psmall[:, 384:512], mask[:, :], ALU.mult))
            gset("psmall", sv_)
            guards["sTm" + str(t)] = (sv_, sv_.n)

        for t in range(NTC):
            gw(TEN, "pw2")
            gw(TEN, "pw3")
            gw(TEN, "psmall")
            gw(TEN, "sTm0")
            gw(TEN, "sTm1" if NTC > 1 else "sTm0")
            gw(TEN, "vT_rdy")
            if not first:
                gw(TEN, "Sbf_rdy")
            started = False
            if not first:
                for j in range(DC):
                    lhs = tcs(q_fm, j, t)
                    TEN.matmul(PW[2][:, :], lhs, S_bf[:, j * D:j * D + 512], start=(j == 0), stop=False)
                    TEN.matmul(PW[3][:, :], lhs, S_bf[:, j * D + 512:(j + 1) * D], start=(j == 0), stop=False)
                    TEN.matmul(psmall[:, t:t + 1], lhs, sv_bf[:, j:j + 1], start=(j == 0), stop=False)
                started = True
            blks = [(0, 0)] if t == 0 else [(128, 0), (256, 1)]
            for bi, (off, b) in enumerate(blks):
                lhs = sTm[:, off:off + 128]
                fstart = (not started) and bi == 0
                last = bi == len(blks) - 1
                TEN.matmul(PW[2][:, :], lhs, vT[b][:, 0:512], start=fstart, stop=last)
                TEN.matmul(PW[3][:, :], lhs, vT[b][:, 512:1024], start=fstart, stop=last)
                mm = TEN.matmul(psmall[:, t:t + 1], lhs, ones_col_b[:, :], start=fstart, stop=last)
            st_.inc(mm)
            wt2(VEC, st_)
            VEC.tensor_scalar(den_r[:, t:t + 1], psmall[:, t:t + 1], DEN_EPS, None, ALU.add)
            vd()
            VEC.reciprocal(den_r[:, t:t + 1], den_r[:, t:t + 1])
            vd()
            VEC.tensor_scalar(attn_tok[t][:, 0:512], PW[2][:, :], den_r[:, t:t + 1], None, ALU.mult)
            sv_.inc(VEC.tensor_scalar(attn_tok[t][:, 512:1024], PW[3][:, :], den_r[:, t:t + 1],
                                      None, ALU.mult))
            gset("pw2", sv_)
            gset("pw3", sv_)
            gset("psmall", sv_)
            guards["attn_tok_rdy" + str(t)] = (sv_, sv_.n)
            if t == NTC - 1:
                guards["q_fm"] = (st_, st_.n)

        gw(TEN, "attn_fm")
        for t in range(NTC):
            gw(TEN, "attn_tok_rdy" + str(t))
            emit_transposes(
                [attn_tok[t][:, i * 128:(i + 1) * 128] for i in range(DC)],
                lambda base, nblk, t=t: attn_fm.rearrange(
                    "p (i f) -> p i f", f=T_TILE)[:, base:base + nblk, t * 128:(t + 1) * 128])
        guards["attn_fm_rdy"] = (ss, ss.n)
        guards["attn_tok"] = (ss, ss.n)

        pbank = {(0, 0): 1, (0, 1): 2, (1, 0): 3, (1, 1): 4}
        for j in range(DC):
            si, dn = load_slab(projR[j])
            TEN.wait_ge(sd.h, dn)
            if j == 0:
                for b in range(1, 5):
                    gw(TEN, f"pw{b}")
                gw(TEN, "attn_fm_rdy")
            for t in range(NTC):
                for h in range(2):
                    mm = TEN.matmul(PW[pbank[(t, h)]][:, :], tcs(attn_fm, j, t),
                                    slab[si][:, h * 512:(h + 1) * 512],
                                    start=(j == 0), stop=(j == DC - 1))
            if j == DC - 1:
                st_.inc(mm)
            slab_guard[si] = (st_, st_.n + (1 if j < DC - 1 else 0))
        guards["attn_fm"] = (st_, st_.n)
        wt2(VEC, st_)
        wt2(VEC, sgp)
        for t in range(NTC):
            for h in range(2):
                VEC.tensor_tensor(attnx_tok[t][:, h * 512:(h + 1) * 512], PW[pbank[(t, h)]][:, :],
                                  bpb[:, h * 512:(h + 1) * 512], ALU.add)
            sv_.inc(VEC.tensor_tensor(attnx_tok[t][:, :], attnx_tok[t][:, :], x_tok[t][:, :], ALU.add))
        for b in range(1, 5):
            gset(f"pw{b}", sv_)
        guards["attnx_rdy"] = (sv_, sv_.n)
        guards["x_free"] = (sv_, sv_.n)

        if tile + 1 < cfg.N_TILES:
            next_xd = ph1_emit(tile + 1)

        for j in range(DC):
            gw(TEN, "pw0")
            gw(TEN, "pw1")
            gw(TEN, "psmall")
            gw(TEN, "kT_rdy")
            gw(TEN, "vT_rdy")
            for h in range(2):
                for t in range(NTC):
                    TEN.matmul(PW[h][:, :], kT[t][:, j * 128:(j + 1) * 128],
                               vT[t][:, h * 512:(h + 1) * 512],
                               start=(t == 0), stop=(t == NTC - 1))
            for t in range(NTC):
                mm = TEN.matmul(psmall[:, 8 + j:9 + j], kT[t][:, j * 128:(j + 1) * 128],
                                ones_col_b[:, :], start=(t == 0), stop=(t == NTC - 1))
            st_.inc(mm)
            wt2(VEC, st_)
            if first:
                VEC.tensor_copy(S_st[:, j * D:j * D + 512], PW[0][:, :])
                VEC.tensor_copy(S_st[:, j * D + 512:(j + 1) * D], PW[1][:, :])
                sv_.inc(VEC.tensor_copy(sv_st[:, j:j + 1], psmall[:, 8 + j:9 + j]))
            else:
                VEC.tensor_tensor(S_st[:, j * D:j * D + 512], S_st[:, j * D:j * D + 512],
                                  PW[0][:, :], ALU.add)
                VEC.tensor_tensor(S_st[:, j * D + 512:(j + 1) * D],
                                  S_st[:, j * D + 512:(j + 1) * D], PW[1][:, :], ALU.add)
                sv_.inc(VEC.tensor_tensor(sv_st[:, j:j + 1], sv_st[:, j:j + 1],
                                          psmall[:, 8 + j:9 + j], ALU.add))
            gset("pw0", sv_)
            gset("pw1", sv_)
            gset("psmall", sv_)
            wt2(SCA, sv_)
            ss.inc(SCA.copy(S_bf[:, j * D:(j + 1) * D], S_st[:, j * D:(j + 1) * D]))
            ss.inc(SCA.copy(sv_bf[:, j:j + 1], sv_st[:, j:j + 1]))
        guards["Sbf_rdy"] = (ss, ss.n)
        guards["kT_free"] = (ss, ss.n)
        guards["vT"] = (st_, st_.n)

        gb = {(0, 0): 1, (0, 1): 2, (1, 0): 3, (1, 1): 4}
        for osp in range(NOS // 2):
            gslabs = []
            for j in range(DC):
                sgi, dng = load_slab(wgR[j, :, osp * 1024:(osp + 1) * 1024])
                gslabs.append((sgi, dng))
            TEN.wait_ge(sd.h, gslabs[-1][1])
            for b in range(1, 5):
                gw(TEN, f"pw{b}")
            gw(TEN, "xq_fm_rdy")
            for j in range(DC):
                for t in range(NTC):
                    lhs = tcs(xq_fm, j, t)
                    for osl in range(2):
                        mm = TEN.matmul(PW[gb[(osl, t)]][:, :], lhs,
                                        slab[gslabs[j][0]][:, osl * 512:(osl + 1) * 512],
                                        start=(j == 0), stop=(j == DC - 1))
            st_.inc(mm)
            for j in range(DC):
                slab_guard[gslabs[j][0]] = (st_, st_.n)
            wt2(SCA, st_)
            gw(SCA, "stats" + str(_par[0]))
            gw(SCA, "sgtA")
            gw(SCA, "sgtB")
            gw(SCA, "sgtC")
            for osl in range(2):
                for t in range(NTC):
                    ss.inc(SCA.activation(sg_t[:, (osl * NTC + t) * 512:(osl * NTC + t + 1) * 512],
                                          PW[gb[(osl, t)]][:, :], AF.Silu, scale=ist(t, 10)))
            uslabs = []
            for j in range(DC):
                sui, dnu = load_slab(wuR[j, :, osp * 1024:(osp + 1) * 1024])
                uslabs.append((sui, dnu))
            TEN.wait_ge(sd.h, uslabs[-1][1])
            wt2(TEN, ss)
            for j in range(DC):
                for t in range(NTC):
                    lhs = tcs(xq_fm, j, t)
                    for osl in range(2):
                        mm = TEN.matmul(PW[gb[(osl, t)]][:, :], lhs,
                                        slab[uslabs[j][0]][:, osl * 512:(osl + 1) * 512],
                                        start=(j == 0), stop=(j == DC - 1))
            st_.inc(mm)
            for j in range(DC):
                slab_guard[uslabs[j][0]] = (st_, st_.n)
            wt2(VEC, st_)
            wt2(VEC, ss)
            for osl in range(2):
                os_ = osp * 2 + osl
                for t in range(NTC):
                    sv_.inc(VEC.tensor_tensor(H_tok[t][:, os_ * 512:(os_ + 1) * 512],
                                              sg_t[:, (osl * NTC + t) * 512:(osl * NTC + t + 1) * 512],
                                              PW[gb[(osl, t)]][:, :], ALU.mult))
            for b in range(1, 5):
                gset(f"pw{b}", sv_)
            gset("sgtA", sv_)
            gset("sgtB", sv_)
            gset("sgtC", sv_)
        guards["xq_fm"] = (st_, st_.n)

        for t in range(NTC):
            VEC.reduce_max(ist(t, 11), H_tok[t][:, :], axis=AX.X, apply_absolute_value=True)
            vd()
            VEC.tensor_tensor(ist(t, 14), ist(t, 11), ist(t, 12), ALU.mult)
            vd()
            VEC.tensor_scalar(ist(t, 14), ist(t, 14), AQ_EPS, None, ALU.max)
            vd()
            VEC.tensor_tensor(ist(t, 13), ist(t, 14), mhat[:, 2:3], ALU.mult)
            VEC.reciprocal(ist(t, 15), ist(t, 14))
            vd()
            VEC.tensor_scalar(ist(t, 13), ist(t, 13), 1.0 / 127.0, None, ALU.mult)
            VEC.tensor_tensor(ist(t, 15), ist(t, 15), ist(t, 12), ALU.mult)
            vd()
            VEC.tensor_scalar(ist(t, 15), ist(t, 15), 127.0, None, ALU.mult)
            sv_.inc(vd())
            VEC.tensor_scalar(H_tok[t][:, :], H_tok[t][:, :], ist(t, 15), None, ALU.mult)
            VEC.tensor_copy(i32s[:, :], H_tok[t][:, :])
            sv_.inc(VEC.tensor_single_scalar(hq_tok[t][:, :], i32s[:, :], 0.0, ALU.add))
        guards["hq_tok_rdy"] = (sv_, sv_.n)

        if tile + 1 < cfg.N_TILES:
            _par[0] = (tile + 1) % 2
            ph2_emit(tile + 1, next_xd)
            _par[0] = tile % 2

        gw(TEN, "hq_tok_rdy")
        gw(TEN, "hq_fm")
        for t in range(NTC):
            emit_transposes(
                [hq_tok[t][:, ic * 128:(ic + 1) * 128] for ic in range(IC)],
                lambda base, nblk, t=t: hq_fm.rearrange(
                    "p (i f) -> p i f", f=T_TILE)[:, base:base + nblk, t * 128:(t + 1) * 128])
        guards["hq_fm_rdy"] = (ss, ss.n)

        dbank = {(0, 0): 1, (0, 1): 2, (1, 0): 3, (1, 1): 4}
        for i in range(IC):
            si, dn = load_slab(wdR[i])
            TEN.wait_ge(sd.h, dn)
            if i == 0:
                for b in range(1, 5):
                    gw(TEN, f"pw{b}")
                gw(TEN, "hq_fm_rdy")
            for t in range(NTC):
                lhs = tcs(hq_fm, i, t)
                for h in range(2):
                    mm = TEN.matmul(PW[dbank[(t, h)]][:, :], lhs,
                                    slab[si][:, h * 512:(h + 1) * 512],
                                    start=(i == 0), stop=(i == IC - 1))
            st_.inc(mm)
            slab_guard[si] = (st_, st_.n)
        guards["hq_fm"] = (st_, st_.n)
        wt2(VEC, st_)
        gw(VEC, "attnx_rdy")
        for t in range(NTC):
            for h in range(2):
                VEC.tensor_scalar(H_tok[t][:, h * 512:(h + 1) * 512],
                                  PW[dbank[(t, h)]][:, :], ist(t, 13), None, ALU.mult)
            sv_.inc(VEC.tensor_tensor(attnx_tok[t][:, :], attnx_tok[t][:, :],
                                      H_tok[t][:, 0:D], ALU.add))
            GPS.wait_ge(sv_.h, sv_.n)
            sgp.inc(GPS.dma_start(y_d[seq, t0 + t * 128:t0 + (t + 1) * 128, :], attnx_tok[t][:, :]), 16)
        for b in range(1, 5):
            gset(f"pw{b}", sv_)
        guards["Hfree"] = (sv_, sv_.n)

    GPS.wait_ge(sgp.h, sgp.n)
    return nc


def kernel(**inputs):
    cfg = Cfg()
    nc = build_program(cfg)
    n_cores = 8
    xfull = np.ascontiguousarray(np.asarray(inputs["L0_attn_x"]))
    in_maps = []
    for c in range(n_cores):
        m = {k: np.ascontiguousarray(np.asarray(v)) for k, v in inputs.items() if k != "L0_attn_x"}
        m["L0_attn_x"] = np.ascontiguousarray(xfull[c * cfg.B_local:(c + 1) * cfg.B_local])
        in_maps.append(m)
    res = run_bass_kernel_spmd(nc, in_maps, core_ids=list(range(n_cores)))
    out = np.concatenate([np.asarray(r["y"]) for r in res.results], axis=0)
    return out


# version 13 (final)
# speedup vs baseline: 1.3320x; 1.0421x over previous
import numpy as np
import ml_dtypes
import concourse.bass as bass
import concourse.mybir as mybir
from concourse.bass_utils import run_bass_kernel_spmd

F32 = mybir.dt.float32
BF16 = mybir.dt.bfloat16
I32 = mybir.dt.int32
AF = mybir.ActivationFunctionType
ALU = mybir.AluOpType
AX = mybir.AxisListType

LN_EPS = 1e-5
DEN_EPS = 1e-6
WQ_EPS = 1e-5
AQ_EPS = 1e-5


class Cfg:
    def __init__(self, B_local=4, T_seq=2048, D=1024, INNER=4096):
        self.B_local = B_local
        self.T_seq = T_seq
        self.D = D
        self.INNER = INNER
        self.DC = D // 128
        self.IC = INNER // 128
        self.T_TILE = 256
        self.NTC = self.T_TILE // 128
        self.TILES_PER_SEQ = T_seq // self.T_TILE
        self.N_TILES = self.B_local * self.TILES_PER_SEQ


def build_program(cfg: Cfg):
    from contextlib import ExitStack
    nc = bass.Bass("TRN2", target_bir_lowering=False)
    stack = ExitStack()
    D, DC, IC, INNER = cfg.D, cfg.DC, cfg.IC, cfg.INNER
    NTC, T_TILE = cfg.NTC, cfg.T_TILE
    NOS = INNER // 512

    x_d = nc.dram_tensor("L0_attn_x", [cfg.B_local, cfg.T_seq, D], F32, kind="ExternalInput").ap()
    y_d = nc.dram_tensor("y", [cfg.B_local, cfg.T_seq, D], F32, kind="ExternalOutput").ap()
    wqkv_d = nc.dram_tensor("L0_attn_w_qkv", [D, 3 * D], F32, kind="ExternalInput").ap()
    bqkv_d = nc.dram_tensor("L0_attn_b_qkv", [3 * D], F32, kind="ExternalInput").ap()
    wgate_d = nc.dram_tensor("L0_attn_w_gate", [D, D], F32, kind="ExternalInput").ap()
    bgate_d = nc.dram_tensor("L0_attn_b_gate", [D], F32, kind="ExternalInput").ap()
    wproj_d = nc.dram_tensor("L0_attn_w_proj", [D, D], F32, kind="ExternalInput").ap()
    bproj_d = nc.dram_tensor("L0_attn_b_proj", [D], F32, kind="ExternalInput").ap()
    lng_d = nc.dram_tensor("L0_attn_ln_g", [D], F32, kind="ExternalInput").ap()
    lnb_d = nc.dram_tensor("L0_attn_ln_b", [D], F32, kind="ExternalInput").ap()
    fg_d = nc.dram_tensor("L0_ffn_w_gate", [INNER, D], F32, kind="ExternalInput").ap()
    fu_d = nc.dram_tensor("L0_ffn_w_up", [INNER, D], F32, kind="ExternalInput").ap()
    fd_d = nc.dram_tensor("L0_ffn_w_down", [D, INNER], F32, kind="ExternalInput").ap()

    qkB = nc.dram_tensor("qkB", [2 * DC, 128, D], BF16, kind="Internal").ap()
    gateB = nc.dram_tensor("gateB", [DC, 128, D], BF16, kind="Internal").ap()
    vR = nc.dram_tensor("vR", [DC, 128, D], BF16, kind="Internal").ap()
    projR = nc.dram_tensor("projR", [DC, 128, D], BF16, kind="Internal").ap()
    wgR = nc.dram_tensor("wgR", [DC, 128, INNER], BF16, kind="Internal").ap()
    wuR = nc.dram_tensor("wuR", [DC, 128, INNER], BF16, kind="Internal").ap()
    wdR = nc.dram_tensor("wdR", [IC, 128, D], BF16, kind="Internal").ap()

    mask_np = np.triu(np.ones((128, 128), np.float32)).astype(ml_dtypes.bfloat16)
    mask_d = nc.inline_tensor(mask_np, name="cmask_c").ap()
    idb_d = nc.inline_tensor(np.eye(128, dtype=np.float32).astype(ml_dtypes.bfloat16), name="idb_c").ap()
    ones_col_b_d = nc.inline_tensor(np.ones((128, 1), np.float32).astype(ml_dtypes.bfloat16), name="onescb").ap()
    ones_row_f_d = nc.inline_tensor(np.ones((1, 128), np.float32), name="onesrf").ap()
    ones_col_f_d = nc.inline_tensor(np.ones((128, 1), np.float32), name="onescf").ap()
    CONST_VALS = [LN_EPS, DEN_EPS, -1.0, 0.5, WQ_EPS, 1.0, 0.0, 127.0]
    consts_np = np.repeat(np.array([CONST_VALS], np.float32), 128, axis=0)
    consts_d = nc.inline_tensor(consts_np, name="consts_c").ap()

    ctx = stack.enter_context
    A = nc.alloc_sbuf_tensor

    consts = A("consts_s", [128, 8], F32)
    mask = A("mask", [128, 128], BF16)
    idb = A("idb_s", [128, 128], BF16)
    ones_col_b = A("ones_col_b", [128, 1], BF16)
    ones_row_f = A("ones_row_f", [1, 128], F32)
    ones_col_f = A("ones_col_f", [128, 1], F32)
    biases = A("biases", [128, 40], F32)
    bq_cols = A("bq_cols", [128, 3 * DC], F32)
    bg_cols = A("bg_cols", [128, DC], F32)
    lngb = A("lngb", [128, D], BF16)
    lnbb = A("lnbb", [128, D], BF16)
    bvb = A("bvb", [128, D], BF16)
    bpb = A("bpb", [128, D], BF16)
    mhat = A("mhat", [128, 3], F32)
    wscale = A("wscale", [128, 1], F32)
    wabs = A("wabs", [128, 33], F32)
    wsc = A("wsc", [1, 8], F32)
    stats = A("stats", [128, 64], F32)
    den_r = A("den_r", [128, 2], F32)
    _par = [0]
    ist = lambda t, c: stats[:, 32 * _par[0] + 16 * t + c:32 * _par[0] + 16 * t + c + 1]

    x_tok = [A(f"x_tok{t}", [128, D], F32) for t in range(NTC)]
    xn2_tok = [A(f"xn2_tok{t}", [128, D], BF16) for t in range(NTC)]
    xq_tok = [A(f"xq_tok{t}", [128, D], BF16) for t in range(NTC)]
    attnx_tok = [A(f"attnx_tok{t}", [128, D], F32) for t in range(NTC)]
    out_tok = attnx_tok

    fmA = A("fmA", [128, 8192], BF16)
    xn2_fm = fmA[:, 0:2048]
    gate_fm = fmA[:, 2048:4096]
    q_fm = fmA[:, 4096:6144]
    attn_fm = fmA[:, 6144:8192]
    hq_fm = fmA[:, 0:8192]
    wtern = fmA[:, 4096:8192]
    xq_fm = A("xq_fm", [128, 2048], BF16)
    k_fm = A("k_fm", [128, 2048], BF16)
    attn_tok = [k_fm[:, 0:1024], k_fm[:, 1024:2048]]
    kT = [A(f"kT{t}", [128, D], BF16) for t in range(NTC)]
    vT = [A(f"vT{t}", [128, D], BF16) for t in range(NTC)]

    S_st = A("S_st", [128, DC * D], F32)
    S_bf = A("S_bf", [128, DC * D], BF16)
    sv_st = A("sv_st", [128, DC], F32)
    sv_bf = A("sv_bf", [128, DC], BF16)

    H_tok = [A(f"H_tok{t}", [128, INNER], F32) for t in range(NTC)]
    xn_tok = [H_tok[t][:, D:2 * D] for t in range(NTC)]
    i32s = A("i32s", [128, 4096], I32)
    hq_tok = [A(f"hq_tok{t}", [128, INNER], BF16) for t in range(NTC)]
    sg_t = A("sg_t", [128, 2048], F32)
    tmp_tok = sg_t[:, 0:D]
    wtile = H_tok[0]
    wtiles = [H_tok[0], H_tok[1]]
    qdummy = hq_tok[0]
    rowv = H_tok[0][0:1, :]
    sTm = xq_tok[0][:, 0:384]

    NSLAB = 9
    slab = [A(f"slab{i}", [128, 1024], BF16) for i in range(NSLAB)]

    trBs = [ctx(nc.psum_tensor("trBa", [128, 1024], BF16)),
            ctx(nc.psum_tensor("trBb", [128, 1024], BF16))]
    PW = [ctx(nc.psum_tensor(f"pw{i}", [128, 512], F32)) for i in range(5)]
    psmall = ctx(nc.psum_tensor("psmall", [128, 512], F32))

    class Sem:
        def __init__(self, name):
            self.h = stack.enter_context(nc.semaphore(name))
            self.n = 0

        def inc(self, instr, k=1):
            instr.then_inc(self.h, k)
            self.n += k
            return instr

    sd = Sem("sd")
    sgp = Sem("sgp")
    sv_ = Sem("sv")
    ss = Sem("ss")
    st_ = Sem("st")

    TEN, VEC, SCA, SYN, GPS = nc.tensor, nc.vector, nc.scalar, nc.sync, nc.gpsimd

    for i, v in enumerate(CONST_VALS):
        nc.const_aps.register(F32, v, consts[:, i:i + 1])
    cEPS = consts[:, 0:1]

    def wt(eng, *sems):
        for s in sems:
            if s.n > 0:
                eng.wait_ge(s.h, s.n)

    def vd():
        return VEC.drain()

    for dst, src in [(consts, consts_d), (mask, mask_d), (idb, idb_d),
                     (ones_col_b, ones_col_b_d), (ones_row_f, ones_row_f_d),
                     (ones_col_f, ones_col_f_d)]:
        sd.inc(SYN.dma_start(dst[:, :], src), 16)
    with nc.allow_non_contiguous_dma("tiny bias transposes"):
        sd.inc(SYN.dma_start(bq_cols[:, :], bqkv_d.rearrange("(c p) -> p c", p=128)), 16)
        sd.inc(SYN.dma_start(bg_cols[:, :], bgate_d.rearrange("(c p) -> p c", p=128)), 16)
    sd.inc(SYN.dma_start(rowv[:, 0:D], lng_d.rearrange("(o d) -> o d", o=1)), 16)
    sd.inc(SYN.dma_start(rowv[:, D:2 * D], lnb_d.rearrange("(o d) -> o d", o=1)), 16)
    sd.inc(SYN.dma_start(rowv[:, 2 * D:3 * D], bqkv_d[2 * D:3 * D].rearrange("(o d) -> o d", o=1)), 16)
    sd.inc(SYN.dma_start(rowv[:, 3 * D:4 * D], bproj_d.rearrange("(o d) -> o d", o=1)), 16)

    wt(VEC, sd)
    VEC.tensor_copy(biases[:, 0:DC], bq_cols[:, 0:DC])
    VEC.tensor_copy(biases[:, 8:8 + DC], bq_cols[:, DC:2 * DC])
    VEC.tensor_scalar(biases[:, 16:16 + DC], bg_cols[:, :], 0.5, None, ALU.mult)
    sv_.inc(VEC.tensor_scalar(biases[:, 24:24 + DC], bq_cols[:, 0:DC], -1.0, None, ALU.mult))

    def bcast_row(dst, col0):
        wt(TEN, sd, sv_, ss)
        TEN.matmul(PW[0][:, :], ones_row_f[:, :], rowv[:, col0:col0 + 512], start=True, stop=True)
        st_.inc(TEN.matmul(PW[1][:, :], ones_row_f[:, :], rowv[:, col0 + 512:col0 + 1024],
                           start=True, stop=True))
        wt(VEC, st_)
        VEC.tensor_copy(dst[:, 0:512], PW[0][:, :])
        sv_.inc(VEC.tensor_copy(dst[:, 512:1024], PW[1][:, :]))

    bcast_row(lngb, 0)
    bcast_row(lnbb, D)
    bcast_row(bvb, 2 * D)
    bcast_row(bpb, 3 * D)

    bufw = [None, None]

    def bwait(eng, i):
        if bufw[i] is not None:
            eng.wait_ge(bufw[i][0].h, bufw[i][1])

    wternw = [None, None]

    def conv_one(src_ap, j, out_cols, store_fn):
        i = j % 2
        bwait(SYN, i)
        wt(SYN, st_)
        sd.inc(SYN.dma_start(wtiles[i][:, 0:out_cols], src_ap[j * 128:(j + 1) * 128, :]), 16)
        VEC.wait_ge(sd.h, sd.n)
        if wternw[i] is not None:
            VEC.wait_ge(wternw[i][0].h, wternw[i][1])
        wtp = wtern[:, i * 2048:i * 2048 + out_cols]
        sv_.inc(VEC.tensor_copy(wtp, wtiles[i][:, 0:out_cols]))
        bufw[i] = (sv_, sv_.n)
        GPS.wait_ge(sv_.h, sv_.n)
        store_fn(wtp)
        wternw[i] = (sgp, sgp.n)

    def conv_blocks(src_ap, in_chunks, out_cols, dst):
        for j in range(in_chunks):
            def store(wtp, j=j):
                dv = dst[:, :, j * 128:(j + 1) * 128]
                sgp.inc(GPS.dma_start(dv.rearrange("c p f -> p c f"),
                                      wtp.rearrange("p (c f) -> p c f", f=128)), 16)
            conv_one(src_ap, j, out_cols, store)

    def conv_rows(src_ap, in_chunks, out_cols, dst):
        for j in range(in_chunks):
            def store(wtp, j=j):
                sgp.inc(GPS.dma_start(dst[j], wtp), 16)
            conv_one(src_ap, j, out_cols, store)

    conv_blocks(wqkv_d[:, 0:2 * D], DC, 2 * D, qkB)
    conv_blocks(wgate_d, DC, D, gateB)
    conv_rows(wqkv_d[:, 2 * D:3 * D], DC, D, vR)
    conv_rows(wproj_d, DC, D, projR)
    bufw[0] = bufw[1] = (sv_, sv_.n)

    def quant_matrix(src_flat_tiles, n_elem, mhat_col):
        ntile = n_elem // (128 * 4096)
        for r in range(ntile):
            i = r % 2
            Q, qs = (SYN, sd) if i == 0 else (GPS, sgp)
            bwait(Q, i)
            wt(Q, st_)
            qs.inc(Q.dma_start(wtiles[i][:, :], src_flat_tiles[r]), 16)
            SCA.wait_ge(qs.h, qs.n)
            ss.inc(SCA.activation(qdummy[:, :], wtiles[i][:, :], AF.Abs,
                                  accum_out=wabs[:, r:r + 1]))
            bufw[i] = (ss, ss.n)
        wt(VEC, ss)
        sv_.inc(VEC.reduce_sum(wabs[:, 32:33], wabs[:, 0:ntile], axis=AX.X))
        wt(TEN, sv_)
        st_.inc(TEN.matmul(psmall[0:1, 0:1], ones_col_f[:, :], wabs[:, 32:33], start=True, stop=True))
        wt(VEC, st_)
        VEC.tensor_scalar(wsc[:, 0:1], psmall[0:1, 0:1], 1.0 / n_elem, WQ_EPS, ALU.mult, ALU.max)
        vd()
        VEC.reciprocal(wsc[:, 1:2], wsc[:, 0:1])
        for _ in range(2):
            vd()
            VEC.tensor_tensor(wsc[:, 2:3], wsc[:, 0:1], wsc[:, 1:2], ALU.mult)
            vd()
            VEC.tensor_scalar(wsc[:, 2:3], wsc[:, 2:3], -1.0, 2.0, ALU.mult, ALU.add)
            vd()
            VEC.tensor_tensor(wsc[:, 1:2], wsc[:, 1:2], wsc[:, 2:3], ALU.mult)
        vd()
        sv_.inc(VEC.tensor_copy(wsc[:, 3:4], wsc[:, 1:2]))
        wt(TEN, sv_)
        TEN.matmul(psmall[:, 2:3], ones_row_f[:, :], wsc[:, 0:1], start=True, stop=True)
        st_.inc(TEN.matmul(psmall[:, 3:4], ones_row_f[:, :], wsc[:, 1:2], start=True, stop=True))
        wt(VEC, st_)
        VEC.tensor_copy(mhat_col[:, 0:1], psmall[:, 2:3])
        sv_.inc(VEC.tensor_copy(wscale[:, :], psmall[:, 3:4]))

    def ternarize_store_rows(src_2d, in_chunks, out_cols, dst):
        OUTR = src_2d.shape[0]
        for r in range(OUTR // 128):
            i = r % 2
            Q, qs = (SYN, sd) if i == 0 else (GPS, sgp)
            bwait(Q, i)
            qs.inc(Q.dma_start(wtiles[i][:, 0:out_cols], src_2d[r * 128:(r + 1) * 128, :]), 16)
            VEC.wait_ge(qs.h, qs.n)
            wt(VEC, st_)
            if r == 0:
                wt(VEC, sgp)
            vd()
            VEC.tensor_scalar(wtiles[i][:, 0:out_cols], wtiles[i][:, 0:out_cols],
                              wscale[:, 0:1], None, ALU.mult)
            VEC.tensor_copy(i32s[:, 0:out_cols], wtiles[i][:, 0:out_cols])
            sv_.inc(VEC.tensor_scalar(wtern[:, 0:out_cols], i32s[:, 0:out_cols], 1.0, -1.0,
                                      ALU.min, ALU.max))
            bufw[i] = (sv_, sv_.n)
            for g8 in range(in_chunks // 8):
                wt(TEN, sv_, ss)
                for i8 in range(8):
                    jc = g8 * 8 + i8
                    mm = TEN.transpose(trBs[0][:, i8 * 128:(i8 + 1) * 128],
                                       wtern[:, jc * 128:(jc + 1) * 128], idb[:, :])
                st_.inc(mm)
                wt(SCA, st_)
                wt(SCA, sgp)
                ss.inc(SCA.copy(hq_fm[:, g8 * 1024:(g8 + 1) * 1024], trBs[0][:, :]))
            wt(GPS, ss)
            dv = dst[:, :, r * 128:(r + 1) * 128]
            sgp.inc(GPS.dma_start(dv.rearrange("j p f -> p j f"),
                                  hq_fm[:, 0:in_chunks * 128].rearrange("p (j f) -> p j f", f=128)), 16)

    quant_matrix(fg_d.rearrange("(r p a) (b) -> r p (a b)", p=128, a=4), INNER * D, mhat[:, 0:1])
    ternarize_store_rows(fg_d, DC, D, wgR)
    quant_matrix(fu_d.rearrange("(r p a) (b) -> r p (a b)", p=128, a=4), INNER * D, mhat[:, 1:2])
    ternarize_store_rows(fu_d, DC, D, wuR)
    quant_matrix(fd_d.rearrange("(r p) (b) -> r p b", p=128), D * INNER, mhat[:, 2:3])
    ternarize_store_rows(fd_d, IC, INNER, wdR)

    def tcs(buf, i, t):
        return buf[:, i * T_TILE + t * 128:i * T_TILE + (t + 1) * 128]

    def fm_dst(buf, t):
        return buf.rearrange("p (i f) -> p i f", f=T_TILE)[:, :, t * 128:(t + 1) * 128]

    guards = {}
    waited = {}

    def gw(eng, key):
        g = guards.get(key)
        if g is None:
            return
        wkey = (id(eng), id(g[0]))
        if waited.get(wkey, -1) >= g[1]:
            return
        eng.wait_ge(g[0].h, g[1])
        waited[wkey] = g[1]

    def gset(key, sem):
        guards[key] = (sem, sem.n)

    def wt2(eng, *sems):
        for s in sems:
            if s.n > 0:
                wkey = (id(eng), id(s))
                if waited.get(wkey, -1) >= s.n:
                    continue
                eng.wait_ge(s.h, s.n)
                waited[wkey] = s.n

    slab_rr = [0]
    slab_guard = [None] * NSLAB

    def load_slab(src_ap):
        i = slab_rr[0] % NSLAB
        slab_rr[0] += 1
        g = slab_guard[i]
        if g is not None:
            wkey = (id(SYN), id(g[0]), "slab", i)
            SYN.wait_ge(g[0].h, g[1])
        sd.inc(SYN.dma_start(slab[i][:, :], src_ap), 16)
        return i, sd.n

    tr_half = [0]

    def emit_transposes(srcs, dst_fn, evac_sem=None):
        n = len(srcs)
        for base in range(0, n, 8):
            nblk = min(8, n - base)
            h = tr_half[0] % 2
            tr_half[0] += 1
            bank = trBs[h]
            gw(TEN, f"trB{h}")
            for bi in range(nblk):
                mm = TEN.transpose(bank[:, bi * 128:(bi + 1) * 128], srcs[base + bi], idb[:, :])
            st_.inc(mm)
            wt2(SCA, st_)
            sem = evac_sem or ss
            sem.inc(SCA.copy(dst_fn(base, nblk),
                             bank[:, 0:nblk * 128].rearrange("p (i f) -> p i f", f=128)))
            gset(f"trB{h}", sem)

    def ph1_emit(tile):
        seq = tile // cfg.TILES_PER_SEQ
        t0 = (tile % cfg.TILES_PER_SEQ) * T_TILE
        gw(SYN, "x_free")
        xd = []
        for t in range(NTC):
            sd.inc(SYN.dma_start(x_tok[t][:, :], x_d[seq, t0 + t * 128:t0 + (t + 1) * 128, :]), 16)
            xd.append(sd.n)
        guards["xdma"] = (sd, sd.n)
        return xd

    def ph2_emit(tile, xd):
        wt2(VEC, st_)
        wt2(SCA, st_)
        gw(SCA, "sgtA")
        gw(SCA, "sgtB")
        for t in range(NTC):
            SCA.wait_ge(sd.h, xd[t])
            VEC.wait_ge(sd.h, xd[t])
            wt2(SCA, sv_)
            ss.inc(SCA.activation(tmp_tok[:, :], x_tok[t][:, :], AF.Square, accum_out=ist(t, 1)))
            VEC.reduce_sum(ist(t, 0), x_tok[t][:, :], axis=AX.X)
            wt2(VEC, ss)
            vd()
            VEC.tensor_scalar(ist(t, 2), ist(t, 0), 1.0 / D, None, ALU.mult)
            VEC.tensor_scalar(ist(t, 14), ist(t, 1), 1.0 / D, None, ALU.mult)
            vd()
            VEC.tensor_tensor(ist(t, 15), ist(t, 2), ist(t, 2), ALU.mult)
            vd()
            VEC.tensor_tensor(ist(t, 14), ist(t, 14), ist(t, 15), ALU.subtract)
            sv_.inc(vd())
            wt2(SCA, sv_)
            ss.inc(SCA.activation(ist(t, 15), ist(t, 14), AF.Sqrt, bias=cEPS))
            wt2(VEC, ss)
            VEC.reciprocal(ist(t, 3), ist(t, 15))
            vd()
            sv_.inc(VEC.tensor_scalar(xn_tok[t][:, :], x_tok[t][:, :], ist(t, 2), ist(t, 3),
                                      ALU.subtract, ALU.mult))
            wt2(SCA, sv_)
            ss.inc(SCA.activation(tmp_tok[:, :], xn_tok[t][:, :], AF.Square, accum_out=ist(t, 5)))
            VEC.reduce_sum(ist(t, 4), xn_tok[t][:, :], axis=AX.X)
            VEC.reduce_max(ist(t, 8), xn_tok[t][:, :], axis=AX.X, apply_absolute_value=True)
            wt2(VEC, ss)
            vd()
            VEC.tensor_scalar(ist(t, 6), ist(t, 4), 1.0 / D, None, ALU.mult)
            VEC.tensor_scalar(ist(t, 14), ist(t, 5), 1.0 / D, None, ALU.mult)
            vd()
            VEC.tensor_tensor(ist(t, 15), ist(t, 6), ist(t, 6), ALU.mult)
            vd()
            VEC.tensor_tensor(ist(t, 14), ist(t, 14), ist(t, 15), ALU.subtract)
            sv_.inc(vd())
            wt2(SCA, sv_)
            ss.inc(SCA.activation(ist(t, 15), ist(t, 14), AF.Sqrt, bias=cEPS))
            wt2(VEC, ss)
            VEC.reciprocal(ist(t, 7), ist(t, 15))
            VEC.tensor_scalar(ist(t, 14), ist(t, 8), AQ_EPS, None, ALU.max)
            vd()
            VEC.reciprocal(ist(t, 15), ist(t, 14))
            vd()
            VEC.tensor_scalar(ist(t, 9), ist(t, 15), 127.0, None, ALU.mult)
            VEC.tensor_tensor(ist(t, 10), ist(t, 14), mhat[:, 0:1], ALU.mult)
            VEC.tensor_tensor(ist(t, 12), ist(t, 14), mhat[:, 1:2], ALU.mult)
            vd()
            VEC.tensor_scalar(ist(t, 10), ist(t, 10), 1.0 / 127.0, None, ALU.mult)
            VEC.tensor_scalar(ist(t, 12), ist(t, 12), 1.0 / 127.0, None, ALU.mult)
            sv_.inc(vd())
            VEC.tensor_scalar(tmp_tok[:, :], xn_tok[t][:, :], ist(t, 6), ist(t, 7),
                              ALU.subtract, ALU.mult)
            VEC.tensor_tensor(tmp_tok[:, :], tmp_tok[:, :], lngb[:, :], ALU.mult)
            sv_.inc(VEC.tensor_tensor(xn2_tok[t][:, :], tmp_tok[:, :], lnbb[:, :], ALU.add))
            VEC.tensor_scalar(tmp_tok[:, :], xn_tok[t][:, :], ist(t, 9), None, ALU.mult)
            VEC.tensor_copy(i32s[:, 0:D], tmp_tok[:, :])
            sv_.inc(VEC.tensor_single_scalar(xq_tok[t][:, :], i32s[:, 0:D], 0.0, ALU.add))
        guards["ph2done"] = (sv_, sv_.n)
        guards["stats" + str(_par[0])] = (sv_, sv_.n)
        gset("sgtA", sv_)
        gset("sgtB", sv_)

    _par[0] = 0
    xd0 = ph1_emit(0)
    ph2_emit(0, xd0)
    next_xd = xd0

    for tile in range(cfg.N_TILES):
        seq = tile // cfg.TILES_PER_SEQ
        ti = tile % cfg.TILES_PER_SEQ
        first = (ti == 0)
        t0 = ti * T_TILE
        _par[0] = tile % 2

        gw(TEN, "ph2done")
        gw(TEN, "xn2_fm")
        gw(TEN, "xq_fm")
        for t in range(NTC):
            for src_tok, dstfm in [(xn2_tok[t], xn2_fm), (xq_tok[t], xq_fm)]:
                emit_transposes(
                    [src_tok[:, i * 128:(i + 1) * 128] for i in range(DC)],
                    lambda base, nblk, dstfm=dstfm, t=t: dstfm.rearrange(
                        "p (i f) -> p i f", f=T_TILE)[:, base:base + nblk,
                                                      t * 128:(t + 1) * 128])
        guards["xn2_fm_rdy"] = (ss, ss.n)
        guards["xq_fm_rdy"] = (ss, ss.n)

        def dout_mm(base, c, bank):
            si, dn = load_slab(base[c])
            TEN.wait_ge(sd.h, dn)
            gw(TEN, f"pw{bank}")
            gw(TEN, "xn2_fm_rdy")
            for j in range(DC):
                mm = TEN.matmul(PW[bank][:, 0:T_TILE], slab[si][:, j * 128:(j + 1) * 128],
                                xn2_fm[:, j * T_TILE:(j + 1) * T_TILE],
                                start=(j == 0), stop=(j == DC - 1))
            st_.inc(mm)
            slab_guard[si] = (st_, st_.n)
            return PW[bank]

        for c in range(DC):
            pb = dout_mm(gateB, c, c % 4)
            wt2(SCA, st_)
            gw(SCA, "sgtC")
            ss.inc(SCA.activation(sg_t[:, 1536:1536 + T_TILE], pb[:, 0:T_TILE], AF.Tanh,
                                  bias=biases[:, 16 + c:17 + c], scale=0.5))
            wt2(VEC, ss)
            sv_.inc(VEC.tensor_scalar(gate_fm[:, c * T_TILE:(c + 1) * T_TILE],
                                      sg_t[:, 1536:1536 + T_TILE], 0.5, 0.5, ALU.mult, ALU.add))
            gset(f"pw{c % 4}", sv_)
            gset("sgtC", sv_)
        for c in range(DC):
            pb = dout_mm(qkB, c, c % 4)
            wt2(SCA, st_)
            rk = "sgtA" if c % 2 == 0 else "sgtB"
            gw(SCA, rk)
            if c == 0:
                gw(VEC, "q_fm")
            soff = (c % 2) * 768
            ss.inc(SCA.activation(sg_t[:, soff:soff + 256], pb[:, 0:T_TILE], AF.Relu,
                                  bias=biases[:, 24 + c:25 + c], scale=-1.0))
            ss.inc(SCA.activation(sg_t[:, soff + 256:soff + 512], sg_t[:, soff:soff + 256],
                                  AF.Exp, scale=-1.0))
            wt2(VEC, ss)
            VEC.tensor_scalar(sg_t[:, soff + 512:soff + 768], pb[:, 0:T_TILE],
                              biases[:, c:c + 1], 0.0, ALU.add, ALU.max)
            sv_.inc(VEC.tensor_tensor(q_fm[:, c * T_TILE:(c + 1) * T_TILE],
                                      sg_t[:, soff + 512:soff + 768],
                                      sg_t[:, soff + 256:soff + 512], ALU.add))
            gset(f"pw{c % 4}", sv_)
            gset(rk, sv_)
        guards["q_fm_rdy"] = (sv_, sv_.n)
        for c in range(DC):
            pb = dout_mm(qkB, DC + c, c % 4)
            wt2(VEC, st_)
            soff = (c % 2) * 768
            rk = "sgtA" if c % 2 == 0 else "sgtB"
            if c == 0:
                gw(VEC, "attn_tok")
                gw(VEC, "k_fm")
            gw(VEC, rk)
            sv_.inc(VEC.scalar_tensor_tensor(sg_t[:, soff:soff + 256], pb[:, 0:T_TILE],
                                             biases[:, 8 + c:9 + c],
                                             gate_fm[:, c * T_TILE:(c + 1) * T_TILE],
                                             ALU.add, ALU.mult))
            gset(f"pw{c % 4}", sv_)
            wt2(SCA, sv_)
            ss.inc(SCA.activation(sg_t[:, soff + 256:soff + 512], sg_t[:, soff:soff + 256],
                                  AF.Relu, scale=-1.0))
            ss.inc(SCA.activation(sg_t[:, soff + 256:soff + 512], sg_t[:, soff + 256:soff + 512],
                                  AF.Exp, scale=-1.0))
            wt2(VEC, ss)
            VEC.tensor_single_scalar(sg_t[:, soff + 512:soff + 768], sg_t[:, soff:soff + 256],
                                     0.0, ALU.max)
            sv_.inc(VEC.tensor_tensor(k_fm[:, c * T_TILE:(c + 1) * T_TILE],
                                      sg_t[:, soff + 512:soff + 768],
                                      sg_t[:, soff + 256:soff + 512], ALU.add))
            gset(rk, sv_)
        guards["k_fm_rdy"] = (sv_, sv_.n)

        vbank = {(0, 0): 1, (0, 1): 2, (1, 0): 3, (1, 1): 4}
        for j in range(DC):
            si, dn = load_slab(vR[j])
            TEN.wait_ge(sd.h, dn)
            if j == 0:
                for b in range(1, 5):
                    gw(TEN, f"pw{b}")
                gw(TEN, "xn2_fm_rdy")
            for t in range(NTC):
                for h in range(2):
                    mm = TEN.matmul(PW[vbank[(t, h)]][:, :], tcs(xn2_fm, j, t),
                                    slab[si][:, h * 512:(h + 1) * 512],
                                    start=(j == 0), stop=(j == DC - 1))
            if j == DC - 1:
                st_.inc(mm)
            slab_guard[si] = (st_, st_.n + (1 if j < DC - 1 else 0))
        guards["xn2_fm"] = (st_, st_.n)
        wt2(VEC, st_)
        for t in range(NTC):
            gw(VEC, "vT")
            VEC.tensor_tensor(vT[t][:, 0:512], PW[vbank[(t, 0)]][:, :], bvb[:, 0:512], ALU.add)
            sv_.inc(VEC.tensor_tensor(vT[t][:, 512:1024], PW[vbank[(t, 1)]][:, :],
                                      bvb[:, 512:1024], ALU.add))
        for b in range(1, 5):
            gset(f"pw{b}", sv_)
        guards["vT_rdy"] = (sv_, sv_.n)

        gw(TEN, "k_fm_rdy")
        gw(TEN, "kT_free")
        for t in range(NTC):
            emit_transposes(
                [tcs(k_fm, i, t) for i in range(DC)],
                lambda base, nblk, t=t: kT[t][:, base * 128:(base + nblk) * 128])
        guards["kT_rdy"] = (ss, ss.n)
        guards["k_fm"] = (st_, st_.n)

        for t in range(NTC):
            gw(TEN, "q_fm_rdy")
            gw(TEN, "k_fm_rdy")
            gw(TEN, "psmall")
            nblk = t + 1
            for b in range(nblk):
                for j in range(DC):
                    mm = TEN.matmul(psmall[:, 256 + b * 128:256 + (b + 1) * 128], tcs(k_fm, j, b),
                                    tcs(q_fm, j, t),
                                    start=(j == 0 and b == 0), stop=(j == DC - 1 and b == nblk - 1))
            st_.inc(mm)
            wt2(VEC, st_)
            if t == 0:
                sv_.inc(VEC.tensor_tensor(sTm[:, 0:128], psmall[:, 256:384], mask[:, :], ALU.mult))
            else:
                VEC.tensor_copy(sTm[:, 128:256], psmall[:, 256:384])
                sv_.inc(VEC.tensor_tensor(sTm[:, 256:384], psmall[:, 384:512], mask[:, :], ALU.mult))
            gset("psmall", sv_)
            guards["sTm" + str(t)] = (sv_, sv_.n)

        for t in range(NTC):
            gw(TEN, "pw2")
            gw(TEN, "pw3")
            gw(TEN, "psmall")
            gw(TEN, "sTm0")
            gw(TEN, "sTm1" if NTC > 1 else "sTm0")
            gw(TEN, "vT_rdy")
            if not first:
                gw(TEN, "Sbf_rdy")
            started = False
            if not first:
                for j in range(DC):
                    lhs = tcs(q_fm, j, t)
                    TEN.matmul(PW[2][:, :], lhs, S_bf[:, j * D:j * D + 512], start=(j == 0), stop=False)
                    TEN.matmul(PW[3][:, :], lhs, S_bf[:, j * D + 512:(j + 1) * D], start=(j == 0), stop=False)
                    TEN.matmul(psmall[:, t:t + 1], lhs, sv_bf[:, j:j + 1], start=(j == 0), stop=False)
                started = True
            blks = [(0, 0)] if t == 0 else [(128, 0), (256, 1)]
            for bi, (off, b) in enumerate(blks):
                lhs = sTm[:, off:off + 128]
                fstart = (not started) and bi == 0
                last = bi == len(blks) - 1
                TEN.matmul(PW[2][:, :], lhs, vT[b][:, 0:512], start=fstart, stop=last)
                TEN.matmul(PW[3][:, :], lhs, vT[b][:, 512:1024], start=fstart, stop=last)
                mm = TEN.matmul(psmall[:, t:t + 1], lhs, ones_col_b[:, :], start=fstart, stop=last)
            st_.inc(mm)
            wt2(VEC, st_)
            VEC.tensor_scalar(den_r[:, t:t + 1], psmall[:, t:t + 1], DEN_EPS, None, ALU.add)
            vd()
            VEC.reciprocal(den_r[:, t:t + 1], den_r[:, t:t + 1])
            vd()
            VEC.tensor_scalar(attn_tok[t][:, 0:512], PW[2][:, :], den_r[:, t:t + 1], None, ALU.mult)
            sv_.inc(VEC.tensor_scalar(attn_tok[t][:, 512:1024], PW[3][:, :], den_r[:, t:t + 1],
                                      None, ALU.mult))
            gset("pw2", sv_)
            gset("pw3", sv_)
            gset("psmall", sv_)
            guards["attn_tok_rdy" + str(t)] = (sv_, sv_.n)
            if t == NTC - 1:
                guards["q_fm"] = (st_, st_.n)

        gw(TEN, "attn_fm")
        for t in range(NTC):
            gw(TEN, "attn_tok_rdy" + str(t))
            emit_transposes(
                [attn_tok[t][:, i * 128:(i + 1) * 128] for i in range(DC)],
                lambda base, nblk, t=t: attn_fm.rearrange(
                    "p (i f) -> p i f", f=T_TILE)[:, base:base + nblk, t * 128:(t + 1) * 128])
        guards["attn_fm_rdy"] = (ss, ss.n)
        guards["attn_tok"] = (ss, ss.n)

        pbank = {(0, 0): 1, (0, 1): 2, (1, 0): 3, (1, 1): 4}
        for j in range(DC):
            si, dn = load_slab(projR[j])
            TEN.wait_ge(sd.h, dn)
            if j == 0:
                for b in range(1, 5):
                    gw(TEN, f"pw{b}")
                gw(TEN, "attn_fm_rdy")
            for t in range(NTC):
                for h in range(2):
                    mm = TEN.matmul(PW[pbank[(t, h)]][:, :], tcs(attn_fm, j, t),
                                    slab[si][:, h * 512:(h + 1) * 512],
                                    start=(j == 0), stop=(j == DC - 1))
            if j == DC - 1:
                st_.inc(mm)
            slab_guard[si] = (st_, st_.n + (1 if j < DC - 1 else 0))
        guards["attn_fm"] = (st_, st_.n)
        wt2(VEC, st_)
        wt2(VEC, sgp)
        for t in range(NTC):
            for h in range(2):
                VEC.tensor_tensor(attnx_tok[t][:, h * 512:(h + 1) * 512], PW[pbank[(t, h)]][:, :],
                                  bpb[:, h * 512:(h + 1) * 512], ALU.add)
            sv_.inc(VEC.tensor_tensor(attnx_tok[t][:, :], attnx_tok[t][:, :], x_tok[t][:, :], ALU.add))
        for b in range(1, 5):
            gset(f"pw{b}", sv_)
        guards["attnx_rdy"] = (sv_, sv_.n)
        guards["x_free"] = (sv_, sv_.n)

        if tile + 1 < cfg.N_TILES:
            next_xd = ph1_emit(tile + 1)

        for j in range(DC):
            gw(TEN, "pw0")
            gw(TEN, "pw1")
            gw(TEN, "psmall")
            gw(TEN, "kT_rdy")
            gw(TEN, "vT_rdy")
            for h in range(2):
                for t in range(NTC):
                    TEN.matmul(PW[h][:, :], kT[t][:, j * 128:(j + 1) * 128],
                               vT[t][:, h * 512:(h + 1) * 512],
                               start=(t == 0), stop=(t == NTC - 1))
            for t in range(NTC):
                mm = TEN.matmul(psmall[:, 8 + j:9 + j], kT[t][:, j * 128:(j + 1) * 128],
                                ones_col_b[:, :], start=(t == 0), stop=(t == NTC - 1))
            st_.inc(mm)
            wt2(VEC, st_)
            if first:
                VEC.tensor_copy(S_st[:, j * D:j * D + 512], PW[0][:, :])
                VEC.tensor_copy(S_st[:, j * D + 512:(j + 1) * D], PW[1][:, :])
                sv_.inc(VEC.tensor_copy(sv_st[:, j:j + 1], psmall[:, 8 + j:9 + j]))
            else:
                VEC.tensor_tensor(S_st[:, j * D:j * D + 512], S_st[:, j * D:j * D + 512],
                                  PW[0][:, :], ALU.add)
                VEC.tensor_tensor(S_st[:, j * D + 512:(j + 1) * D],
                                  S_st[:, j * D + 512:(j + 1) * D], PW[1][:, :], ALU.add)
                sv_.inc(VEC.tensor_tensor(sv_st[:, j:j + 1], sv_st[:, j:j + 1],
                                          psmall[:, 8 + j:9 + j], ALU.add))
            gset("pw0", sv_)
            gset("pw1", sv_)
            gset("psmall", sv_)
            wt2(SCA, sv_)
            ss.inc(SCA.copy(S_bf[:, j * D:(j + 1) * D], S_st[:, j * D:(j + 1) * D]))
            ss.inc(SCA.copy(sv_bf[:, j:j + 1], sv_st[:, j:j + 1]))
        guards["Sbf_rdy"] = (ss, ss.n)
        guards["kT_free"] = (ss, ss.n)
        guards["vT"] = (st_, st_.n)

        gb = {(0, 0): 1, (0, 1): 2, (1, 0): 3, (1, 1): 4}
        for osp in range(NOS // 2):
            gslabs = []
            for j in range(DC):
                sgi, dng = load_slab(wgR[j, :, osp * 1024:(osp + 1) * 1024])
                gslabs.append((sgi, dng))
            TEN.wait_ge(sd.h, gslabs[-1][1])
            for b in range(1, 5):
                gw(TEN, f"pw{b}")
            gw(TEN, "xq_fm_rdy")
            for j in range(DC):
                for t in range(NTC):
                    lhs = tcs(xq_fm, j, t)
                    for osl in range(2):
                        mm = TEN.matmul(PW[gb[(osl, t)]][:, :], lhs,
                                        slab[gslabs[j][0]][:, osl * 512:(osl + 1) * 512],
                                        start=(j == 0), stop=(j == DC - 1))
            st_.inc(mm)
            for j in range(DC):
                slab_guard[gslabs[j][0]] = (st_, st_.n)
            wt2(SCA, st_)
            gw(SCA, "stats" + str(_par[0]))
            gw(SCA, "sgtA")
            gw(SCA, "sgtB")
            gw(SCA, "sgtC")
            for osl in range(2):
                for t in range(NTC):
                    ss.inc(SCA.activation(sg_t[:, (osl * NTC + t) * 512:(osl * NTC + t + 1) * 512],
                                          PW[gb[(osl, t)]][:, :], AF.Silu, scale=ist(t, 10)))
            uslabs = []
            for j in range(DC):
                sui, dnu = load_slab(wuR[j, :, osp * 1024:(osp + 1) * 1024])
                uslabs.append((sui, dnu))
            TEN.wait_ge(sd.h, uslabs[-1][1])
            wt2(TEN, ss)
            for j in range(DC):
                for t in range(NTC):
                    lhs = tcs(xq_fm, j, t)
                    for osl in range(2):
                        mm = TEN.matmul(PW[gb[(osl, t)]][:, :], lhs,
                                        slab[uslabs[j][0]][:, osl * 512:(osl + 1) * 512],
                                        start=(j == 0), stop=(j == DC - 1))
            st_.inc(mm)
            for j in range(DC):
                slab_guard[uslabs[j][0]] = (st_, st_.n)
            wt2(VEC, st_)
            wt2(VEC, ss)
            for osl in range(2):
                os_ = osp * 2 + osl
                for t in range(NTC):
                    sv_.inc(VEC.tensor_tensor(H_tok[t][:, os_ * 512:(os_ + 1) * 512],
                                              sg_t[:, (osl * NTC + t) * 512:(osl * NTC + t + 1) * 512],
                                              PW[gb[(osl, t)]][:, :], ALU.mult))
            for b in range(1, 5):
                gset(f"pw{b}", sv_)
            gset("sgtA", sv_)
            gset("sgtB", sv_)
            gset("sgtC", sv_)
        guards["xq_fm"] = (st_, st_.n)

        for t in range(NTC):
            VEC.reduce_max(ist(t, 11), H_tok[t][:, :], axis=AX.X, apply_absolute_value=True)
            vd()
            VEC.tensor_tensor(ist(t, 14), ist(t, 11), ist(t, 12), ALU.mult)
            vd()
            VEC.tensor_scalar(ist(t, 14), ist(t, 14), AQ_EPS, None, ALU.max)
            vd()
            VEC.tensor_tensor(ist(t, 13), ist(t, 14), mhat[:, 2:3], ALU.mult)
            VEC.reciprocal(ist(t, 15), ist(t, 14))
            vd()
            VEC.tensor_scalar(ist(t, 13), ist(t, 13), 1.0 / 127.0, None, ALU.mult)
            VEC.tensor_tensor(ist(t, 15), ist(t, 15), ist(t, 12), ALU.mult)
            vd()
            VEC.tensor_scalar(ist(t, 15), ist(t, 15), 127.0, None, ALU.mult)
            sv_.inc(vd())
            VEC.tensor_scalar(H_tok[t][:, :], H_tok[t][:, :], ist(t, 15), None, ALU.mult)
            VEC.tensor_copy(i32s[:, :], H_tok[t][:, :])
            sv_.inc(VEC.tensor_single_scalar(hq_tok[t][:, :], i32s[:, :], 0.0, ALU.add))
        guards["hq_tok_rdy"] = (sv_, sv_.n)

        if tile + 1 < cfg.N_TILES:
            _par[0] = (tile + 1) % 2
            ph2_emit(tile + 1, next_xd)
            _par[0] = tile % 2

        gw(TEN, "hq_tok_rdy")
        gw(TEN, "hq_fm")
        for t in range(NTC):
            emit_transposes(
                [hq_tok[t][:, ic * 128:(ic + 1) * 128] for ic in range(IC)],
                lambda base, nblk, t=t: hq_fm.rearrange(
                    "p (i f) -> p i f", f=T_TILE)[:, base:base + nblk, t * 128:(t + 1) * 128])
        guards["hq_fm_rdy"] = (ss, ss.n)

        dbank = {(0, 0): 1, (0, 1): 2, (1, 0): 3, (1, 1): 4}
        for i in range(IC):
            si, dn = load_slab(wdR[i])
            TEN.wait_ge(sd.h, dn)
            if i == 0:
                for b in range(1, 5):
                    gw(TEN, f"pw{b}")
                gw(TEN, "hq_fm_rdy")
            for t in range(NTC):
                lhs = tcs(hq_fm, i, t)
                for h in range(2):
                    mm = TEN.matmul(PW[dbank[(t, h)]][:, :], lhs,
                                    slab[si][:, h * 512:(h + 1) * 512],
                                    start=(i == 0), stop=(i == IC - 1))
            st_.inc(mm)
            slab_guard[si] = (st_, st_.n)
        guards["hq_fm"] = (st_, st_.n)
        wt2(VEC, st_)
        gw(VEC, "attnx_rdy")
        for t in range(NTC):
            for h in range(2):
                VEC.tensor_scalar(H_tok[t][:, h * 512:(h + 1) * 512],
                                  PW[dbank[(t, h)]][:, :], ist(t, 13), None, ALU.mult)
            sv_.inc(VEC.tensor_tensor(attnx_tok[t][:, :], attnx_tok[t][:, :],
                                      H_tok[t][:, 0:D], ALU.add))
            GPS.wait_ge(sv_.h, sv_.n)
            sgp.inc(GPS.dma_start(y_d[seq, t0 + t * 128:t0 + (t + 1) * 128, :], attnx_tok[t][:, :]), 16)
        for b in range(1, 5):
            gset(f"pw{b}", sv_)
        guards["Hfree"] = (sv_, sv_.n)

    GPS.wait_ge(sgp.h, sgp.n)
    return nc


def kernel(**inputs):
    cfg = Cfg()
    nc = build_program(cfg)
    n_cores = 8
    xfull = np.ascontiguousarray(np.asarray(inputs["L0_attn_x"]))
    in_maps = []
    for c in range(n_cores):
        m = {k: np.ascontiguousarray(np.asarray(v)) for k, v in inputs.items() if k != "L0_attn_x"}
        m["L0_attn_x"] = np.ascontiguousarray(xfull[c * cfg.B_local:(c + 1) * cfg.B_local])
        in_maps.append(m)
    res = run_bass_kernel_spmd(nc, in_maps, core_ids=list(range(n_cores)))
    out = np.concatenate([np.asarray(r["y"]) for r in res.results], axis=0)
    return out
